# version 15 (final)
# baseline (speedup 1.0000x reference)
import contextlib
import numpy as np
import ml_dtypes

import concourse.bass as bass
import concourse.mybir as mybir
from concourse.bass_utils import run_bass_kernel_spmd

F32 = mybir.dt.float32
BF16 = mybir.dt.bfloat16
AF = mybir.ActivationFunctionType
ALU = mybir.AluOpType
AX = mybir.AxisListType
BF = ml_dtypes.bfloat16

N_CORES = 8
D = 1024
T_FULL = 16384
L = T_FULL // N_CORES
NB = L // 128
TL = 512
NT = L // TL
KR = 512

DEBUG = False
S_DRAIN_LIMIT = 10**9
S_DRAIN_SET = None


class Plan:
    ENGINES = ("T", "V", "S", "G", "Y")

    def __init__(self):
        self.ops = {e: [] for e in self.ENGINES}
        self.count = {e: 0 for e in self.ENGINES}
        for s in ("CC", "Y0", "Y1", "Y2", "Y3", "G0", "G1", "G2", "G3"):
            self.count[s] = 0

    def emit(self, eng, fn, waits=(), inc=0, inc_sem=None, tag=""):
        flat = []
        for w in waits:
            if isinstance(w, list):
                flat += w
            else:
                flat.append(w)
        waits = [(s, v) for (s, v) in flat if v > 0]
        for sem, val in waits:
            assert val <= self.count[sem], (
                f"wait on future {sem}={val} (now {self.count[sem]}) at {tag}")
        sem = inc_sem or eng
        self.ops[eng].append((waits, fn, inc, sem, tag))
        if inc:
            self.count[sem] += inc
        return self.count[sem]


class Bank:

    def __init__(self, ap):
        self.ap = ap
        self.prod = ("T", 0)
        self.cons = []

    def prod_waits(self):
        w, self.cons = self.cons, []
        return w

    def set_prod(self, val):
        self.prod = ("T", val)

    def cons_waits(self):
        return [self.prod]

    def add_cons(self, eng, val):
        self.cons.append((eng, val))


def build_nc(debug=False):
    nc = bass.Bass("TRN2", target_bir_lowering=False)

    def din(name, shape, dt=F32):
        return nc.dram_tensor(name, shape, dt, kind="ExternalInput").ap()

    x_in = din("x", [L, D])
    wq_im = din("wq_im", [128, 8 * D])
    wk_im = din("wk_im", [128, 8 * D])
    wo_im = din("wo_im", [128, 8 * D])
    rw1_im = din("rw1_im", [128, 8 * D])
    winv_im = din("winv_im", [128, 8 * D])
    vf_im = din("vf_im", [128, 8 * KR])
    wf_im = din("wf_im", [128, 8 * KR])
    uf_im = din("uf_im", [128, 4 * D])
    wie_im = din("wie_im", [128, 8 * KR])
    vi_im = din("vi_im", [128, 8 * KR])
    ui_im = din("ui_im", [128, 4 * D])
    wqkv_im = din("wqkv_im", [128, 8 * 384])
    wout_im = din("wout_im", [128, D])
    bf_im = din("bf_im", [8, D])
    bi_im = din("bi_im", [8, D])
    rw2_im = din("rw2_im", [128, 64])
    gam_in = din("gam", [128, 8])
    b1g_in = din("b1g", [128, 8])
    tmpr_in = din("tmpr", [128, 1])
    alph_in = din("alph", [128, 1])
    b2_in = din("b2c", [8, 1])
    eb_in = din("ebc", [8, 1])
    pmsk_in = din("pmsk", [128, 8])
    rcp_in = din("rcp", [128, 16])
    madd_in = din("madd", [128, 128])
    idf_in = din("idf", [128, 128])
    idb_in = din("idb", [128, 128], BF16)
    onef_in = din("onef", [128, 512])
    onesb_in = din("onesb", [128, 512], BF16)
    esel_in = din("esel", [8, 512])

    out_d = nc.dram_tensor("out", [L, D], F32, kind="ExternalOutput").ap()
    dbg = {}
    if debug:
        def dout(name, shape, dt=F32):
            dbg[name] = nc.dram_tensor("dbg_" + name, shape, dt,
                                       kind="ExternalOutput").ap()
        dout("xnfm", [128, 8 * L], BF16)
        dout("gt", [128, 144])
        dout("att", [128, 128])
        dout("oatt", [128, L], BF16)
        dout("prf", [128, 8])
        dout("m1", [128, 8 * KR], BF16)
        dout("m2", [128, 8 * D], BF16)
        dout("qfm", [128, 8 * TL], BF16)
        dout("kfm", [128, 8 * TL], BF16)
        dout("cumf", [128, 4 * TL])
        dout("cumi", [128, 4 * TL])
        dout("wfm", [8, 2 * TL], BF16)
        dout("preb", [128, 8 * TL], BF16)
        dout("initf", [128, 16])
        dout("grd", [128, 192])
        dout("xnt", [128, 2048], BF16)

    scr_wk = nc.dram_tensor("scr_wk", [128, 8 * D], BF16).ap()
    scr_rw1 = nc.dram_tensor("scr_rw1", [128, 8 * D], BF16).ap()
    scr_wo = nc.dram_tensor("scr_wo", [128, 8 * D], BF16).ap()
    scr_m2 = nc.dram_tensor("scr_m2", [128, 8 * D], BF16).ap()
    scr_vf = nc.dram_tensor("scr_vf", [128, 8 * KR], BF16).ap()
    scr_wf = nc.dram_tensor("scr_wf", [128, 8 * KR], BF16).ap()
    scr_uf = nc.dram_tensor("scr_uf", [128, 4 * D], BF16).ap()
    scr_ui = nc.dram_tensor("scr_ui", [128, 4 * D], BF16).ap()
    scr_m1 = nc.dram_tensor("scr_m1", [128, 8 * KR], BF16).ap()
    scr_m3 = nc.dram_tensor("scr_m3", [128, 8 * KR], BF16).ap()
    ccg_in = nc.dram_tensor("ccg_in", [128, 144], F32).ap()
    ccg_out = nc.dram_tensor("ccg_out", [8, 128, 144], F32).ap()

    st = contextlib.ExitStack()
    with st:
        def sbuf(name, shape, dt=F32):
            return st.enter_context(nc.sbuf_tensor(name, shape, dt))

        CUMF = sbuf("CUMF", [128, 4 * TL])
        CUMI = sbuf("CUMI", [128, 4 * TL])
        XBLK0 = sbuf("XBLK0", [128, D])
        XBLK1 = sbuf("XBLK1", [128, D])
        OUTB0 = sbuf("OUTB0", [128, D])
        OUTB1 = sbuf("OUTB1", [128, D])
        GRD = sbuf("GRD", [128, 192])
        SML = sbuf("SML", [128, 160])
        STG = sbuf("STG", [128, 144])
        AGR = sbuf("AGR", [128, 8 * 144])
        GT = sbuf("GT", [128, 144])
        ATT = sbuf("ATT", [128, 128])
        LG = sbuf("LG", [128, 512])
        WFM32 = sbuf("WFM32", [8, 512])
        WTM = sbuf("WTM", [128, 56])
        ATT2 = sbuf("ATT2", [128, 128])
        PTMP = sbuf("PTMP", [128, 4 * 512], BF16)
        PRF = sbuf("PRF", [128, 8])
        BRW = sbuf("BRW", [8, 1])
        ONES = sbuf("ONES", [128, 512])
        IDF = sbuf("IDF", [128, 128])
        MADD = sbuf("MADD", [128, 128])
        GAM = sbuf("GAM", [128, 8])
        B1G = sbuf("B1G", [128, 8])
        TMPR = sbuf("TMPR", [128, 1])
        ALPH = sbuf("ALPH", [128, 1])
        PMSK = sbuf("PMSK", [128, 8])
        RCP = sbuf("RCP", [128, 16])
        ESEL = sbuf("ESEL", [8, 512])

        XNFM = sbuf("XNFM", [128, 8 * L], BF16)
        VFM = sbuf("VFM", [128, L], BF16)
        OATT = sbuf("OATT", [128, L], BF16)
        SLOTA = sbuf("SLOTA", [128, 8 * D], BF16)
        SLOTB = sbuf("SLOTB", [128, 8 * D], BF16)
        SLOTC = sbuf("SLOTC", [128, 8 * KR], BF16)
        SLOTD = sbuf("SLOTD", [128, 8 * KR], BF16)
        SLOTE = sbuf("SLOTE", [128, 8 * D], BF16)
        QFM = sbuf("QFM", [128, 8 * TL], BF16)
        KFM = sbuf("KFM", [128, 8 * TL], BF16)
        PRODF = sbuf("PRODF", [128, 4 * TL], BF16)
        PRODI = sbuf("PRODI", [128, 4 * TL], BF16)
        HCH = sbuf("HCH", [128, 1024], BF16)
        PREB = sbuf("PREB", [128, 8 * TL], BF16)
        WQKV = sbuf("WQKV", [128, 8 * 384], BF16)
        WOUT = sbuf("WOUT", [128, D], BF16)
        BFT = sbuf("BFT", [8, D], BF16)
        BIT = sbuf("BIT", [8, D], BF16)
        RW2B = sbuf("RW2B", [128, 64], BF16)
        WFMB = sbuf("WFMB", [8, 2 * TL], BF16)
        XNT = QFM[:, 0:2048]
        PRFB = sbuf("PRFB", [128, 8], BF16)
        ATB = sbuf("ATB", [128, 128], BF16)
        IDB = sbuf("IDB", [128, 128], BF16)
        ONESB = sbuf("ONESB", [128, 512], BF16)

        PS = [st.enter_context(nc.psum_tensor(f"ps{i}", [128, 512], F32))
              for i in range(6)]
        PSB = [st.enter_context(nc.psum_tensor(f"psb{i}", [128, 1024], BF16))
               for i in range(2)]

        sems = {s: st.enter_context(nc.semaphore(f"sem_{s}"))
                for s in ("T", "V", "S", "CC",
                          "Y0", "Y1", "Y2", "Y3", "G0", "G1", "G2", "G3")}

        p = Plan()
        B = [Bank(ps) for ps in PS]
        BB = [Bank(ps) for ps in PSB]

        def t_mm(out_ap, lhsT, rhs, start, stop, waits=(), inc=0, tag=""):
            return p.emit(
                "T",
                lambda t, o=out_ap, a=lhsT, b=rhs, s0=start, s1=stop:
                    t.matmul(o, a, b, start=s0, stop=s1, skip_group_check=True),
                waits, inc, tag=tag)

        def t_tp(out_ap, in_ap, ident, waits=(), inc=0, tag=""):
            return p.emit(
                "T",
                lambda t, o=out_ap, i=in_ap, d=ident: t.transpose(o, i, d),
                waits, inc, tag=tag)

        def v_op(fn, waits=(), inc=0, tag=""):
            return p.emit("V", fn, waits, inc, tag=tag)

        def s_op(fn, waits=(), inc=0, tag=""):
            return p.emit("S", fn, waits, inc, tag=tag)

        def _dma(eng, ring, out_ap, in_ap, waits, tag, chain=True):
            w = list(waits)
            prev = p.count[ring]
            if chain and prev:
                w.append((ring, prev))
            val = p.emit(
                eng,
                lambda e, o=out_ap, i=in_ap: e.dma_start(out=o, in_=i),
                w, 16, inc_sem=ring, tag=tag)
            return (ring, val)

        _slot_ring = [0]

        def g_dma(out_ap, in_ap, waits=(), tag="", ring=None):
            if ring is None:
                if "<-" in tag:
                    ring = "G1" if _slot_ring[0] % 2 == 0 else "G2"
                    _slot_ring[0] += 1
                else:
                    ring = "G0"
            return _dma("G", ring, out_ap, in_ap, waits, tag)

        def g_dma2(out_ap, in_ap, waits=(), tag=""):
            half = out_ap.shape[-1] // 2
            t1 = _dma("G", "G1", out_ap[:, 0:half], in_ap[:, 0:half],
                      waits, tag + " lo")
            t2 = _dma("G", "G2", out_ap[:, half:], in_ap[:, half:],
                      waits, tag + " hi")
            return [t1, t2]

        def y_dma(out_ap, in_ap, waits=(), tag="", ring="Y0", chain=True):
            return _dma("Y", ring, out_ap, in_ap, waits, tag, chain=chain)

        def mm_group(bank, pairs, out_ap=None, extra_waits=(), tag=""):
            o = out_ap if out_ap is not None else bank.ap[:, :]
            n = len(pairs)
            tv = 0
            for i, (a, b) in enumerate(pairs):
                w = (list(extra_waits) + bank.prod_waits()) if i == 0 else []
                tv = t_mm(o, a, b, i == 0, i == n - 1, waits=w,
                          inc=1 if i == n - 1 else 0, tag=f"{tag}.{i}")
            bank.set_prod(tv)
            return tv

        d_winv_a = g_dma(SLOTA[:, 0:4 * D], winv_im[:, 0:4 * D],
                         ring="G0", tag="winv->A lo")
        d_winv_b = g_dma(SLOTA[:, 4 * D:8 * D], winv_im[:, 4 * D:8 * D],
                         ring="G3", tag="winv->A hi")
        d_wie = g_dma(SLOTC[:, :], wie_im[:, :], ring="G2", tag="wie->C")
        d_rw1s = g_dma(SLOTB[:, :], rw1_im[:, :], ring="G1",
                       waits=[d_wie], tag="rw1->B")
        d_vi = g_dma(SLOTD[:, :], vi_im[:, :], ring="G0", tag="vi->D")
        wqkv_dd = []
        for cc in range(8):
            wqkv_dd.append(g_dma(
                PREB[:, cc * 384:(cc + 1) * 384],
                wqkv_im[:, cc * 384:(cc + 1) * 384],
                tag=f"wqkv stage {cc}", ring="G0" if cc % 2 == 0 else "G3"))
        d_slote = g_dma(SLOTE[:, :], wq_im[:, :], ring="G3", tag="slote<-wq")

        cY = 0
        for dst, src in ((ONES, onef_in), (IDF, idf_in), (MADD, madd_in),
                         (GAM, gam_in), (B1G, b1g_in), (TMPR, tmpr_in),
                         (ALPH, alph_in), (PMSK, pmsk_in), (RCP, rcp_in),
                         (IDB, idb_in), (ONESB, onesb_in), (ESEL, esel_in)):
            cY = y_dma(dst[:, :], src[:, :], tag="const", ring="Y2",
                       chain=False)
        d_b2 = y_dma(SML[0:8, 0:1], b2_in[:, :], tag="b2")
        d_eb = y_dma(SML[0:8, 1:2], eb_in[:, :], tag="eb")
        v_brw = v_op(lambda v: v.tensor_add(BRW[0:8, 0:1], SML[0:8, 0:1],
                                            SML[0:8, 1:2]),
                     waits=[d_eb], inc=1, tag="brw")

        d_bi = y_dma(XBLK0[0:8, 0:D], bi_im[:, :], tag="bi stage")
        v_bi = v_op(lambda v: v.tensor_scalar(BIT[0:8, :], XBLK0[0:8, 0:D],
                                              ALPH[0:8, 0:1], None, ALU.mult),
                    waits=[d_bi, cY], inc=1, tag="bi*alpha")
        fam_w = {"m1": [d_winv_a, d_winv_b, d_wie],
                 "m3": [d_winv_a, d_winv_b, d_vi],
                 "m2": [d_winv_a, d_winv_b, d_rw1s]}
        dump_hist = []
        npre = 0
        for name, rhs_t, scr, nhalf in (("m1", SLOTC, scr_m1, 1),
                                        ("m3", SLOTD, scr_m3, 1),
                                        ("m2", SLOTB, scr_m2, 2)):
            for jc in range(8):
                for half in range(nhalf):
                    bank = B[4 + (npre % 2)]
                    hreg = HCH[:, (npre % 2) * 512:(npre % 2 + 1) * 512]
                    pairs = [(SLOTA[:, ic * D + jc * 128:
                                    ic * D + (jc + 1) * 128],
                              rhs_t[:, ic * (512 * nhalf) + half * 512:
                                    ic * (512 * nhalf) + (half + 1) * 512])
                             for ic in range(8)]
                    tv = mm_group(bank, pairs, extra_waits=fam_w[name],
                                  tag=f"{name} {jc}.{half}")
                    wv = bank.cons_waits()
                    if len(dump_hist) >= 2:
                        wv = wv + [dump_hist[-2]]
                    ev = s_op(lambda s, bank=bank, hreg=hreg:
                              s.copy(hreg, bank.ap[:, :]),
                              waits=wv, inc=1, tag=f"{name} ev {jc}.{half}")
                    bank.add_cons("S", ev)
                    dv = g_dma(scr[:, jc * 512 * nhalf + half * 512:
                                   jc * 512 * nhalf + (half + 1) * 512],
                               hreg, waits=[("S", ev)], ring="G3",
                               tag=f"{name} dump {jc}.{half}")
                    dump_hist.append(dv)
                    npre += 1
        t_m_done = p.count["T"]
        g_m_done = ("G3", p.count["G3"])

        xbufs = [OUTB0, OUTB1]
        buf_cons = {0: (0, 0), 1: (0, 0)}
        last_sqred = 0
        for b in range(NB):
            buf = xbufs[b % 2]
            pv, ps_ = buf_cons[b % 2]
            dx = y_dma(buf[:, :], x_in[b * 128:(b + 1) * 128, :],
                       waits=[("V", pv), ("S", ps_)], tag=f"x1 {b}",
                       ring="Y1" if b % 2 == 0 else "Y3")
            vm = v_op(lambda v, b=b, buf=buf: v.reduce_sum(
                GRD[:, b:b + 1], buf[:, :], axis=AX.X),
                waits=[dx], inc=1, tag=f"msum {b}")
            v_op(lambda v, buf=buf: v.tensor_mul(
                CUMF[:, 0:D], buf[:, :], buf[:, :]), tag=f"sq {b}")
            last_sqred = v_op(lambda v, b=b: v.reduce_sum(
                GRD[:, 16 + b:17 + b], CUMF[:, 0:D], axis=AX.X),
                inc=1, tag=f"sqred {b}")
            buf_cons[b % 2] = (last_sqred, 0)
        v_allstats = last_sqred

        def G_(a, b):
            return GRD[:, a:b]
        v_op(lambda v: v.tensor_scalar(G_(32, 48), G_(0, 16), 1.0 / D, None,
                                       ALU.mult),
             waits=[("V", v_allstats)], tag="m")
        v_op(lambda v: v.tensor_scalar(G_(128, 144), G_(16, 32), 1.0 / D,
                                       None, ALU.mult), tag="Ex2")
        v_op(lambda v: v.tensor_mul(G_(144, 160), G_(32, 48), G_(32, 48)),
             tag="m2")
        v_op(lambda v: v.tensor_sub(G_(48, 64), G_(128, 144), G_(144, 160)),
             tag="var")
        v_op(lambda v: v.tensor_scalar(G_(160, 176), G_(48, 64), 1e-5, None,
                                       ALU.add), tag="v+eps")
        v_rec = v_op(lambda v: v.reciprocal(G_(176, 192), G_(160, 176)),
                     inc=1, tag="recip")
        s_s = s_op(lambda s: s.activation(G_(64, 80), G_(176, 192), AF.Sqrt),
                   waits=[("V", v_rec)], inc=1, tag="s=sqrt")
        v_op(lambda v: v.tensor_mul(G_(128, 144), G_(32, 48), G_(64, 80)),
             waits=[("S", s_s)], tag="ms")
        v_negms = v_op(lambda v: v.tensor_scalar(G_(80, 96), G_(128, 144),
                                                 -1.0, None, ALU.mult),
                       inc=1, tag="negms")
        v_op(lambda v: v.tensor_mul(G_(144, 160), G_(48, 64), G_(176, 192)),
             tag="ms2")
        v_op(lambda v: v.tensor_scalar(G_(160, 176), G_(144, 160), 1e-6,
                                       None, ALU.add), tag="+eps2")
        v_r = v_op(lambda v: v.reciprocal(G_(128, 144), G_(160, 176)),
                   inc=1, tag="rr")
        s_r = s_op(lambda s: s.activation(G_(96, 112), G_(128, 144), AF.Sqrt),
                   waits=[("V", v_r)], inc=1, tag="r=sqrt")
        v_r2 = s_r

        v_qkv = 0
        for cc in range(8):
            v_qkv = v_op(lambda v, cc=cc: v.tensor_scalar(
                WQKV[:, cc * 384:(cc + 1) * 384],
                PREB[:, cc * 384:(cc + 1) * 384],
                GAM[:, cc:cc + 1], None, ALU.mult),
                waits=[wqkv_dd[cc], cY], inc=1, tag=f"wqkv*g {cc}")

        ACC = B[3]
        xb2 = [XBLK0, XBLK1, OUTB0]
        last_x2c = {0: v_bi, 1: 0, 2: v_allstats}
        qkv_banks = [B[0], B[1]]
        xnfm_ev = {}
        xn_guard = {0: 0, 1: 0}
        vtp_guard = 0
        s_vfm = 0
        x2_tok = {}
        for b in range(NB):
            buf = xb2[b % 3]
            dx = y_dma(buf[:, :], x_in[b * 128:(b + 1) * 128, :],
                       waits=[("V", last_x2c[b % 3])], tag=f"x2 {b}",
                       ring="Y1" if b % 2 == 0 else "Y3")
            x2_tok[b] = dx
            xn_t = XNT[:, (b % 2) * 1024:(b % 2 + 1) * 1024]
            w = [dx, ("V", v_negms), ("S", s_r),
                 ("T", xn_guard[b % 2])]
            s_xn = s_op(lambda s, buf=buf, xt=xn_t, b=b: s.activation(
                xt, buf[:, :], AF.Identity, bias=GRD[:, 80 + b:81 + b],
                scale=GRD[:, 64 + b:65 + b]),
                waits=w, inc=1, tag=f"xn {b}")
            ev = 0
            for rnd in range(2):
                tvs = 0
                for i in range(4):
                    c = rnd * 4 + i
                    w = ([("S", s_xn), cY] + BB[0].prod_waits()
                         if i == 0 else [])
                    tvs = t_tp(PSB[0][:, c * 128:(c + 1) * 128],
                               XNT[:, (b % 2) * 1024 + c * 128:
                                   (b % 2) * 1024 + (c + 1) * 128],
                               IDB[:, :], waits=w,
                               inc=1 if i == 3 else 0, tag=f"tp {b}.{c}")
                BB[0].set_prod(tvs)
                for i in range(4):
                    c = rnd * 4 + i
                    ev = v_op(lambda v, b=b, c=c: v.tensor_copy(
                        XNFM[:, c * L + b * 128:c * L + (b + 1) * 128],
                        PSB[0][:, c * 128:(c + 1) * 128]),
                        waits=BB[0].cons_waits() if i == 0 else [],
                        inc=1, tag=f"tpev {b}.{c}")
                BB[0].add_cons("V", ev)
            xnfm_ev[b] = ev
            tsx = 0
            for c in range(8):
                tsx = t_mm(ACC.ap[:, 136 + c:137 + c],
                           XNT[:, (b % 2) * 1024 + c * 128:
                               (b % 2) * 1024 + (c + 1) * 128],
                           ONESB[:, 0:1], b == 0, b == NB - 1,
                           waits=([("S", s_xn)] + ([cY] if b == 0 else []))
                           if c == 0 else [],
                           inc=1 if c == 7 else 0, tag=f"sumxn {b}.{c}")
            qb = qkv_banks[b % 2]
            pairs = [(XNFM[:, c * L + b * 128:c * L + (b + 1) * 128],
                      WQKV[:, c * 384:(c + 1) * 384]) for c in range(8)]
            tq = mm_group(qb, pairs, out_ap=qb.ap[:, 0:384],
                          extra_waits=[("V", xnfm_ev[b]), ("V", v_qkv)],
                          tag=f"qkv {b}")
            xn_guard[b % 2] = tsx
            sq = s_op(lambda s, qb=qb, b=b: s.activation(
                PRODF[:, 0:128], qb.ap[:, 0:128], AF.Copy,
                scale=GRD[:, 96 + b:97 + b]),
                waits=qb.cons_waits() + [("T", vtp_guard)],
                inc=1, tag=f"qtm {b}")
            sk = s_op(lambda s, qb=qb, b=b: s.activation(
                PRODF[:, 128:256], qb.ap[:, 128:256], AF.Copy,
                scale=GRD[:, 96 + b:97 + b]), inc=1, tag=f"ktm {b}")
            sv = s_op(lambda s, qb=qb, b=b: s.activation(
                PRODF[:, 256:384], qb.ap[:, 256:384], AF.Copy,
                scale=GRD[:, 96 + b:97 + b]), inc=1, tag=f"vtm {b}")
            vq2 = v_op(lambda v, qb=qb, b=b: v.scalar_tensor_tensor(
                CUMI[:, 0:128], qb.ap[:, 0:128], GRD[:, 96 + b:97 + b],
                PRODF[:, 0:128], ALU.mult, ALU.mult),
                waits=qb.cons_waits() + [("S", s_r), ("T", vtp_guard),
                                         ("S", sv)],
                inc=1, tag=f"q2 {b}")
            vk2 = v_op(lambda v, qb=qb, b=b: v.scalar_tensor_tensor(
                CUMI[:, 128:256], qb.ap[:, 128:256], GRD[:, 96 + b:97 + b],
                PRODF[:, 128:256], ALU.mult, ALU.mult),
                waits=[("S", sk)],
                inc=1, tag=f"k2 {b}")
            qb.add_cons("S", sv)
            qb.add_cons("V", vk2)
            t_mm(ACC.ap[:, 0:128], PRODF[:, 0:128], PRODF[:, 128:256],
                 b == 0, b == NB - 1, waits=[("S", sk)], tag=f"gram {b}")
            t_mm(ACC.ap[:, 128:129], CUMI[:, 0:128], ONES[:, 0:1],
                 b == 0, b == NB - 1,
                 waits=[("V", vq2), cY], tag=f"ssq_q {b}")
            t_mm(ACC.ap[:, 129:130], CUMI[:, 128:256], ONES[:, 0:1],
                 b == 0, b == NB - 1, waits=[("V", vk2)], tag=f"ssq_k {b}")
            vtp_guard = t_tp(PSB[1][:, 0:128], PRODF[:, 256:384], IDB[:, :],
                             waits=[("S", sv)] + BB[1].prod_waits(),
                             inc=1, tag=f"vtp {b}")
            BB[1].set_prod(vtp_guard)
            s_vfm = s_op(lambda s, b=b: s.copy(VFM[:, b * 128:(b + 1) * 128],
                                               PSB[1][:, 0:128]),
                         waits=BB[1].cons_waits(), inc=1, tag=f"vfm {b}")
            BB[1].add_cons("S", s_vfm)
            last_x2c[b % 3] = vk2
        ACC.set_prod(p.count["T"])
        xc_guard = dict(last_x2c)

        cast_gate = x2_tok[11]
        d_wk = g_dma(scr_wk[:, :], wk_im[:, :], ring="G2",
                     waits=[cast_gate], tag="cast wk")
        d_vf = g_dma(scr_vf[:, :], vf_im[:, :], ring="G0",
                     waits=[cast_gate], tag="cast vf")
        d_wf = g_dma(scr_wf[:, :], wf_im[:, :], ring="G1", tag="cast wf")
        d_uf = g_dma(scr_uf[:, :], uf_im[:, :], ring="G2", tag="cast uf")
        d_wo = g_dma(scr_wo[:, :], wo_im[:, :], ring="G0", tag="cast wo")
        d_rw2 = g_dma(RW2B[:, :], rw2_im[:, :], ring="G1", tag="rw2 cast")
        d_bf = g_dma(BFT[:, :], bf_im[:, :], ring="G2", tag="bf cast")
        d_wout = g_dma(WOUT[:, :], wout_im[:, :], ring="G1", tag="wout cast")
        d_ui = g_dma(scr_ui[:, :], ui_im[:, :], ring="G2", tag="cast ui")
        d_rw1dump = g_dma(scr_rw1[:, :], SLOTB[:, :],
                          waits=[("T", t_m_done)], ring="G3",
                          tag="rw1 dump")

        bigslot = {"A": SLOTA, "B": SLOTB}
        slot_load = {}
        slot_load["A"] = g_dma(SLOTA[:, :], scr_rw1[:, :],
                               waits=[("T", t_m_done), d_rw1dump],
                               tag="A<-rw1")
        slot_load["B"] = g_dma(SLOTB[:, :], scr_wk[:, :],
                               waits=[d_rw1dump, d_wk], tag="B<-wk")
        slot_load["C"] = g_dma(SLOTC[:, :], scr_vf[:, :],
                               waits=[("T", t_m_done), d_vf], tag="C<-vf")
        slot_load["D"] = g_dma(SLOTD[:, :], scr_wf[:, :],
                               waits=[("T", t_m_done), d_wf], tag="D<-wf")

        v_stg = v_op(lambda v: v.tensor_copy(STG[:, 0:144], ACC.ap[:, 0:144]),
                     waits=ACC.cons_waits(), inc=1, tag="stage")
        ACC.add_cons("V", v_stg)
        d_ccin = g_dma(ccg_in[:, :], STG[:, :], waits=[("V", v_stg)],
                       tag="cc in")
        cc1 = p.emit(
            "G",
            lambda g: g.collective_compute(
                "AllGather", ALU.bypass,
                replica_groups=[list(range(N_CORES))],
                ins=[ccg_in[:, :]], outs=[ccg_out[:, :, :]]),
            waits=[d_ccin], inc=1, inc_sem="CC", tag="allgather")
        d_agr = g_dma(AGR.rearrange("p (r c) -> p r c", r=8),
                      ccg_out.rearrange("r p c -> p r c"),
                      waits=[("CC", cc1)], tag="cc back")
        vgt = v_op(lambda v: v.tensor_add(GT[:, 0:144], AGR[:, 0:144],
                                          AGR[:, 144:288]),
                   waits=[d_agr], inc=1, tag="gt01")
        pp = [GT, STG]
        for r in range(2, 8):
            srcb, dstb = pp[(r - 2) % 2], pp[(r - 1) % 2]
            vgt = v_op(lambda v, r=r, srcb=srcb, dstb=dstb: v.tensor_add(
                dstb[:, 0:144], srcb[:, 0:144], AGR[:, r * 144:(r + 1) * 144]),
                inc=1, tag=f"gt{r}")
        vprf = v_op(lambda v: v.tensor_scalar(
            PRF[:, :], AGR[:, 136:144], PMSK[:, 0:1], None, ALU.mult),
            waits=[cY], inc=1, tag="prf0")
        pbufs = [PRF[:, 0:8], SML[:, 40:48]]
        for r in range(1, 8):
            sb_, db_ = pbufs[(r - 1) % 2], pbufs[r % 2]
            vprf = v_op(lambda v, r=r, sb_=sb_, db_=db_: v.scalar_tensor_tensor(
                db_, AGR[:, r * 144 + 136:r * 144 + 144],
                PMSK[:, r:r + 1], sb_, ALU.mult, ALU.add),
                inc=1, tag=f"prf{r}")
        vprf = v_op(lambda v: v.tensor_copy(PRF[:, 0:8], pbufs[1]),
                    inc=1, tag="prf fin")
        vprfb = v_op(lambda v: v.tensor_copy(PRFB[:, :], PRF[:, :]),
                     inc=1, tag="prfb")

        def emit_half(t, which, phys):
            dst = QFM if which == "Q" else KFM
            slot = SLOTE if which == "Q" else bigslot[phys]
            sq_ev = 0
            for oc in range(8):
                bank = B[[0, 1, 4, 5][oc % 4]]
                pairs = [(slot[:, dc * D + oc * 128: dc * D + (oc + 1) * 128],
                          XNFM[:, dc * L + t * TL: dc * L + (t + 1) * TL])
                         for dc in range(8)]
                xw = [d_slote if which == "Q" else slot_load[phys]]
                if t == 0 and which == "Q" and oc == 0:
                    xw.append(("V", xnfm_ev[NB - 1]))
                mm_group(bank, pairs, extra_waits=xw, tag=f"t{t}{which}{oc}")
                sq_ev = s_op(lambda s, dst=dst, oc=oc, bank=bank: s.copy(
                    dst[:, oc * TL:(oc + 1) * TL], bank.ap[:, :]),
                    waits=bank.cons_waits(), inc=1, tag=f"t{t}{which}{oc} ev")
                bank.add_cons("S", sq_ev)
            return sq_ev, p.count["T"]

        _, tq0 = emit_half(0, "Q", "A")
        s_qk0, tk0 = emit_half(0, "K", "B")


        s_nq = s_op(lambda s: s.activation(SML[:, 2:3], GT[:, 128:129],
                                           AF.Sqrt),
                    waits=[("V", vgt)], inc=1, tag="nq")
        s_nk = s_op(lambda s: s.activation(SML[:, 3:4], GT[:, 129:130],
                                           AF.Sqrt), inc=1, tag="nk")
        v_op(lambda v: v.tensor_scalar_max(SML[:, 10:11], SML[:, 2:3], 1e-12),
             waits=[("S", s_nk)], tag="nq max")
        v_op(lambda v: v.tensor_scalar_max(SML[:, 11:12], SML[:, 3:4], 1e-12),
             tag="nk max")
        v_op(lambda v: v.reciprocal(SML[:, 4:5], SML[:, 10:11]), tag="invnq")
        v_nk = v_op(lambda v: v.reciprocal(SML[:, 5:6], SML[:, 11:12]),
                    inc=1, tag="invnk")
        s_et = s_op(lambda s: s.activation(SML[:, 6:7], TMPR[:, 0:1], AF.Exp),
                    waits=[cY], inc=1, tag="exp temp")
        v_rs = v_op(lambda v: v.tensor_mul(SML[:, 7:8], SML[:, 6:7],
                                           SML[:, 4:5]),
                    waits=[("S", s_et)], inc=1, tag="rowscale")

        exch = {"s_oatt": 0, "v_initf": 0}

        def emit_exchange_T():
            t_cs1 = mm_group(B[5], [(SML[:, 5:6], IDF[:, :])],
                             out_ap=PS[5][0:1, 0:128],
                             extra_waits=[("V", v_nk), cY], tag="invnk row")
            v_cs = v_op(lambda v: v.tensor_copy(SML[0:1, 32:160],
                                                PS[5][0:1, 0:128]),
                        waits=B[5].cons_waits(), inc=1, tag="invnk row ev")
            B[5].add_cons("V", v_cs)
            t_cs2 = mm_group(B[5], [(ONES[0:1, 0:128], SML[0:1, 32:160])],
                             out_ap=PS[5][:, 0:128],
                             extra_waits=[("V", v_cs)], tag="col bcast")
            v_op(lambda v: v.tensor_scalar(ATT[:, :], GT[:, 0:128],
                                           SML[:, 7:8], None, ALU.mult),
                 waits=[("V", v_rs)] + B[5].cons_waits(), tag="sim row")
            v_simc = v_op(lambda v: v.tensor_mul(ATT2[:, :], ATT[:, :],
                                                 PS[5][:, 0:128]),
                          inc=1, tag="sim col")
            B[5].add_cons("V", v_simc)
            v_op(lambda v: v.tensor_add(ATT[:, :], ATT2[:, :], MADD[:, :]),
                 tag="sim mask")
            v_mx = v_op(lambda v: v.reduce_max(SML[:, 0:1], ATT[:, :],
                                               axis=AX.X, negate=True),
                        inc=1, tag="att max")
            s_ex = s_op(lambda s: s.activation(ATT2[:, :], ATT[:, :], AF.Exp,
                                               bias=SML[:, 0:1]),
                        waits=[("V", v_mx)], inc=1, tag="att exp")
            v_op(lambda v: v.reduce_sum(SML[:, 1:2], ATT2[:, :], axis=AX.X),
                 waits=[("S", s_ex)], tag="att sum")
            v_op(lambda v: v.reciprocal(SML[:, 8:9], SML[:, 1:2]),
                 tag="att rec")
            v_att = v_op(lambda v: v.tensor_scalar(ATT[:, :], ATT2[:, :],
                                                   SML[:, 8:9], None,
                                                   ALU.mult),
                         inc=1, tag="att norm")
            t_atp = mm_group(B[4], [(ATT[:, :], IDF[:, :])],
                             out_ap=PS[4][:, 0:128],
                             extra_waits=[("V", v_att)], tag="att tp")
            s_atb = s_op(lambda s: s.copy(ATB[:, :], PS[4][:, 0:128]),
                         waits=B[4].cons_waits(), inc=1, tag="atb")
            B[4].add_cons("S", s_atb)

            tpk = 0
            for dc in range(8):
                w = ([slot_load["B"], ("V", vprfb)] + B[2].prod_waits()
                     if dc == 0 else [])
                for ec in range(8):
                    tpk = t_mm(PS[2][:, dc:dc + 1],
                               SLOTB[:, ec * D + dc * 128:
                                     ec * D + (dc + 1) * 128],
                               PRFB[:, ec:ec + 1], ec == 0, ec == 7,
                               waits=w if ec == 0 else [],
                               inc=1 if (dc == 7 and ec == 7) else 0,
                               tag=f"pk {dc}.{ec}")
            B[2].set_prod(tpk)
            v_pk = v_op(lambda v: v.tensor_copy(SML[:, 16:24], PS[2][:, 0:8]),
                        waits=B[2].cons_waits(), inc=1, tag="pk ev")
            v_pkb = v_op(lambda v: v.tensor_copy(PRFB[:, :], SML[:, 16:24]),
                         inc=1, tag="pk bf")
            B[2].add_cons("V", v_pkb)
            tinit = 0
            for kc in range(4):
                w = ([("V", v_pkb)] + B[2].prod_waits() if kc == 0 else [])
                for dc in range(8):
                    tinit = t_mm(PS[2][:, 8 + kc:9 + kc],
                                 SLOTD[:, dc * 512 + kc * 128:
                                       dc * 512 + (kc + 1) * 128],
                                 PRFB[:, dc:dc + 1], dc == 0, dc == 7,
                                 waits=w if dc == 0 else [],
                                 inc=1 if (kc == 3 and dc == 7) else 0,
                                 tag=f"initf {kc}.{dc}")
            B[2].set_prod(tinit)
            v_initf = v_op(lambda v: v.tensor_copy(SML[:, 24:28],
                                                   PS[2][:, 8:12]),
                           waits=B[2].cons_waits(), inc=1, tag="initf ev")
            B[2].add_cons("V", v_initf)
            exch["v_initf"] = v_initf
            exch["v_pkb"] = v_pkb

            s_oatt = 0
            for j in range(4):
                bank = B[[0, 1, 4, 5][j % 4]]
                mm_group(bank, [(ATB[:, :], VFM[:, j * 512:(j + 1) * 512])],
                         extra_waits=[("S", s_atb), ("S", s_vfm)],
                         tag=f"oatt {j}")
                s_oatt = s_op(lambda s, j=j, bank=bank: s.copy(
                    OATT[:, j * 512:(j + 1) * 512], bank.ap[:, :]),
                    waits=bank.cons_waits(), inc=1, tag=f"oatt ev {j}")
                bank.add_cons("S", s_oatt)
            exch["s_oatt"] = s_oatt
            slot_load["B"] = g_dma(SLOTB[:, :], scr_m2[:, :],
                                   waits=[("T", tpk), g_m_done],
                                   tag="t0 B<-m2")

        if debug:
            y_dma(dbg["xnfm"], XNFM[:, :], waits=[("V", xnfm_ev[NB - 1])],
                  tag="dbg xnfm")
            y_dma(dbg["grd"], GRD[:, :], waits=[("S", s_r)], tag="dbg grd")
            y_dma(dbg["xnt"], XNT[:, :], waits=[("V", xnfm_ev[NB - 1])],
                  tag="dbg xnt")
            y_dma(dbg["gt"], GT[:, :], waits=[("V", vprfb)], tag="dbg gt")
            y_dma(dbg["prf"], PRF[:, :], waits=[("V", vprfb)], tag="dbg prf")
            g_dma(dbg["m1"], scr_m1[:, :], waits=[("G0", p.count["G0"])],
                  tag="dbg m1")
            g_dma(dbg["m2"], scr_m2[:, :], waits=[("G0", p.count["G0"])],
                  tag="dbg m2")

        out_guard = {0: ("Y2", 0), 1: ("Y2", 0)}
        xb_guard = {0: xc_guard[0], 1: xc_guard[1]}
        v_initi = 0
        for t in range(NT):
            qphys = "A" if t % 2 == 0 else "B"
            kphys = "B" if t % 2 == 0 else "A"
            qslot = bigslot[qphys]
            kslot = bigslot[kphys]
            scan_mark = {"f": [0] * 4, "i": [0] * 4}
            if t == 0:
                s_qk = s_qk0
            else:
                _, tq_b = emit_half(t, "Q", qphys)
                s_qk, tk_b = emit_half(t, "K", kphys)
                slot_load[kphys] = g_dma(kslot[:, :], scr_m2[:, :],
                                         waits=[("T", tk_b), g_m_done],
                                         tag=f"t{t} {kphys}<-m2")

            def yw_scan(br, kc):
                bank = B[2 + (kc % 2)]
                pairs = [(SLOTD[:, dc * 512 + kc * 128:
                                dc * 512 + (kc + 1) * 128],
                          KFM[:, dc * TL:(dc + 1) * TL]) for dc in range(8)]
                mm_group(bank, pairs,
                         extra_waits=[("S", s_qk), slot_load["D"]],
                         tag=f"t{t} yW{br}{kc}")
                CUM = CUMF if br == "f" else CUMI
                if t == 0:
                    ia = 0.0
                    wi = []
                else:
                    ia = CUM[:, kc * TL + TL - 1: kc * TL + TL]
                    wi = []
                sv = v_op(lambda v, CUM=CUM, kc=kc, bank=bank, ia=ia:
                          v.tensor_tensor_scan(
                              CUM[:, kc * TL:(kc + 1) * TL], ONES[:, :],
                              bank.ap[:, :], ia, ALU.mult, ALU.add),
                          waits=bank.cons_waits() + wi + [cY],
                          inc=1, tag=f"t{t} scan{br}{kc}")
                bank.add_cons("V", sv)
                scan_mark[br][kc] = sv

            tlg = [0]

            def h_group(br, fc, slot, sl_key):
                bank = B[[0, 1, 5][fc % 3]]
                pairs = [(slot[:, dc * D + fc * 128: dc * D + (fc + 1) * 128],
                          QFM[:, dc * TL:(dc + 1) * TL]) for dc in range(8)]
                mm_group(bank, pairs, extra_waits=[slot_load[sl_key]],
                         tag=f"t{t} h{br}{fc}")
                hh = HCH[:, (fc % 2) * 512:(fc % 2 + 1) * 512]
                ev = s_op(lambda s, hh=hh, bank=bank, fc=fc: s.activation(
                    hh, bank.ap[:, :], AF.Gelu, bias=B1G[:, fc:fc + 1]),
                    waits=bank.cons_waits() + [cY] +
                          ([("T", tlg[0])] if fc >= 2 else []),
                    inc=1, tag=f"t{t} gelu{br}{fc}")
                bank.add_cons("S", ev)
                lgw = [("S", ev)]
                if fc == 0:
                    lgw += B[4].prod_waits() + [d_rw2]
                tlg[0] = t_mm(PS[4][0:8, :], RW2B[:, fc * 8:(fc + 1) * 8],
                              hh, fc == 0, fc == 7, waits=lgw,
                              inc=1 if fc == 7 else 0, tag=f"t{t} lg{br}{fc}")
                if fc == 7:
                    B[4].set_prod(tlg[0])

            def xv_group(br, kc):
                bank = B[kc % 2]
                pairs = [(SLOTC[:, dc * 512 + kc * 128:
                                dc * 512 + (kc + 1) * 128],
                          QFM[:, dc * TL:(dc + 1) * TL]) for dc in range(8)]
                mm_group(bank, pairs, extra_waits=[slot_load["C"]],
                         tag=f"t{t} xV{br}{kc}")
                return bank

            for i in range(8):
                if i < 4:
                    yw_scan("f", i)
                h_group("f", i, qslot, qphys)
            t_hf = p.count["T"]
            vlg_f = v_op(lambda v: v.tensor_scalar(
                LG[0:8, :], PS[4][0:8, :], BRW[0:8, 0:1], None, ALU.add),
                waits=B[4].cons_waits() + [("V", v_brw)], inc=1,
                tag=f"t{t} lgf ev")
            B[4].add_cons("V", vlg_f)
            if t == 0:
                emit_exchange_T()
                for kc in range(4):
                    sv = v_op(lambda v, kc=kc: v.tensor_scalar(
                        CUMF[:, kc * TL:(kc + 1) * TL],
                        CUMF[:, kc * TL:(kc + 1) * TL],
                        SML[:, 24 + kc:25 + kc], None, ALU.add),
                        waits=[("V", exch["v_initf"]),
                               ("V", scan_mark["f"][kc])],
                        inc=1, tag=f"t0 addinitf{kc}")
                    scan_mark["f"][kc] = sv
            slot_load["D"] = g_dma(SLOTD[:, :], scr_m3[:, :],
                                   waits=[("T", p.count["T"]), g_m_done],
                                   tag=f"t{t} D<-m3")
            if t == 0:
                tini = 0
                for kc in range(4):
                    w = ([slot_load["D"], ("V", exch["v_pkb"])] +
                         B[2].prod_waits() if kc == 0 else [])
                    for dc in range(8):
                        tini = t_mm(PS[2][:, 12 + kc:13 + kc],
                                    SLOTD[:, dc * 512 + kc * 128:
                                          dc * 512 + (kc + 1) * 128],
                                    PRFB[:, dc:dc + 1], dc == 0, dc == 7,
                                    waits=w if dc == 0 else [],
                                    inc=1 if (kc == 3 and dc == 7) else 0,
                                    tag=f"initi {kc}.{dc}")
                B[2].set_prod(tini)
                v_initi = v_op(lambda v: v.tensor_copy(
                    SML[:, 28:32], PS[2][:, 12:16]),
                    waits=B[2].cons_waits(), inc=1, tag="initi ev")
                B[2].add_cons("V", v_initi)

            def softmax_T(br, blk, vlg, sxx_prev):
                w = [("V", vlg)] + (B[4].prod_waits() if blk == 0 else
                                    [("S", sxx_prev)])
                ttp = t_mm(PS[4][:, 128 + blk * 8:136 + blk * 8],
                           LG[0:8, blk * 128:(blk + 1) * 128],
                           IDF[0:8, 0:8], True, True,
                           waits=w, inc=1, tag=f"t{t} lgtp{br}{blk}")
                vmx = v_op(lambda v, blk=blk: v.reduce_max(
                    WTM[:, 8 + blk:9 + blk],
                    PS[4][:, 128 + blk * 8:136 + blk * 8],
                    axis=AX.X, negate=True),
                    waits=[("T", ttp)], inc=1, tag=f"t{t} wmax{br}{blk}")
                sxx = s_op(lambda s, blk=blk: s.activation(
                    WTM[:, 0:8], PS[4][:, 128 + blk * 8:136 + blk * 8],
                    AF.Exp, bias=WTM[:, 8 + blk:9 + blk]),
                    waits=[("V", vmx)] +
                          ([("T", tpb[0])] if blk else []),
                    inc=1, tag=f"t{t} wexp{br}{blk}")
                v_op(lambda v: v.reduce_sum(WTM[:, 12:13], WTM[:, 0:8],
                                            axis=AX.X),
                     waits=[("S", sxx)], tag=f"t{t} wsum{br}{blk}")
                v_op(lambda v: v.reciprocal(WTM[:, 13:14], WTM[:, 12:13]),
                     tag=f"t{t} wrec{br}{blk}")
                vw = v_op(lambda v, blk=blk, t=t: v.tensor_scalar(
                    WTM[:, 24 + blk * 8:32 + blk * 8], WTM[:, 0:8],
                    WTM[:, 13:14],
                    RCP[:, t * 4 + blk:t * 4 + blk + 1],
                    ALU.mult, ALU.mult),
                    inc=1, tag=f"t{t} wnorm{br}{blk}")
                return vw, sxx

            def softmax_TP(br, blk, vw):
                w = [("V", vw)] + (B[5].prod_waits() if blk == 0 else [])
                tpb[0] = t_mm(PS[5][0:8, blk * 128:(blk + 1) * 128],
                              WTM[:, 24 + blk * 8:32 + blk * 8],
                              IDF[:, :], True, True,
                              waits=w, inc=1, tag=f"t{t} wtpb{br}{blk}")
                if blk == 3:
                    B[5].set_prod(tpb[0])

            tpb = [0]
            vws = []
            sxx_last = 0
            for blk in range(4):
                vw, sxx_last = softmax_T("f", blk, vlg_f, sxx_last)
                vws.append(vw)
                bank_x = xv_group("f", blk)
                vp1 = v_op(lambda v, bank_x=bank_x, kc=blk:
                           v.tensor_mul(PTMP[:, kc * TL:(kc + 1) * TL],
                                        bank_x.ap[:, :],
                                        CUMF[:, kc * TL:(kc + 1) * TL]),
                           waits=bank_x.cons_waits() +
                                 [("V", scan_mark["f"][blk])],
                           inc=1, tag=f"t{t} prod1f{blk}")
                bank_x.add_cons("V", vp1)
            B[4].add_cons("S", sxx_last)
            for blk in range(4):
                softmax_TP("f", blk, vws[blk])
                yw_scan("i", blk)
            if t == 0:
                for kc in range(4):
                    sv = v_op(lambda v, kc=kc: v.tensor_scalar(
                        CUMI[:, kc * TL:(kc + 1) * TL],
                        CUMI[:, kc * TL:(kc + 1) * TL],
                        SML[:, 28 + kc:29 + kc], None, ALU.add),
                        waits=[("V", v_initi), ("V", scan_mark["i"][kc])],
                        inc=1, tag=f"t0 addiniti{kc}")
                    scan_mark["i"][kc] = sv
            v1f = v_op(lambda v: v.tensor_copy(WFM32[0:8, :],
                                               PS[5][0:8, 0:512]),
                       waits=B[5].cons_waits(), inc=1, tag=f"t{t} wfm32 f")
            v2f = v_op(lambda v: v.tensor_copy(WFMB[0:8, 0:TL],
                                               PS[5][0:8, 0:512]),
                       inc=1, tag=f"t{t} wfmb f")
            B[5].add_cons("V", v2f)
            slot_load[qphys] = g_dma(qslot[:, :], scr_wo[:, :],
                                     waits=[("T", t_hf), d_wo],
                                     tag=f"t{t} {qphys}<-wo")
            slot_load["C"] = g_dma(SLOTC[:, :], scr_m1[:, :],
                                   waits=[("T", p.count["T"]), g_m_done],
                                   tag=f"t{t} C<-m1")

            for kc in range(4):
                bank_w = B[4 if kc % 2 == 0 else 5]
                mm_group(bank_w, [(ESEL[0:8, kc * 128:(kc + 1) * 128],
                                   WFM32[0:8, :])],
                         extra_waits=[("V", v1f), cY],
                         tag=f"t{t} wexpf{kc}")
                vp2 = v_op(lambda v, kc=kc, bank_w=bank_w:
                           v.tensor_mul(PRODF[:, kc * TL:(kc + 1) * TL],
                                        PTMP[:, kc * TL:(kc + 1) * TL],
                                        bank_w.ap[:, :]),
                           waits=bank_w.cons_waits(), inc=1,
                           tag=f"t{t} prod2f{kc}")
                bank_w.add_cons("V", vp2)
            for i in range(8):
                h_group("i", i, kslot, kphys)
            t_hi = p.count["T"]
            vlg_i = v_op(lambda v: v.tensor_scalar(
                LG[0:8, :], PS[4][0:8, :], BRW[0:8, 0:1], None, ALU.add),
                waits=B[4].cons_waits() + [("V", v_brw)], inc=1,
                tag=f"t{t} lgi ev")
            B[4].add_cons("V", vlg_i)
            if t < NT - 1:
                slot_load[kphys] = g_dma(kslot[:, :], scr_rw1[:, :],
                                         waits=[("T", t_hi), d_rw1dump],
                                         tag=f"t{t} {kphys}<-rw1'")

            tpb = [0]
            vws = []
            sxx_last = 0
            for blk in range(4):
                vw, sxx_last = softmax_T("i", blk, vlg_i, sxx_last)
                vws.append(vw)
                bank_x = xv_group("i", blk)
                vp1 = v_op(lambda v, bank_x=bank_x, kc=blk:
                           v.tensor_mul(PTMP[:, kc * TL:(kc + 1) * TL],
                                        bank_x.ap[:, :],
                                        CUMI[:, kc * TL:(kc + 1) * TL]),
                           waits=bank_x.cons_waits() +
                                 [("V", scan_mark["i"][blk])],
                           inc=1, tag=f"t{t} prod1i{blk}")
                bank_x.add_cons("V", vp1)
            B[4].add_cons("S", sxx_last)
            for blk in range(4):
                softmax_TP("i", blk, vws[blk])
            v1i = v_op(lambda v: v.tensor_copy(WFM32[0:8, :],
                                               PS[5][0:8, 0:512]),
                       waits=B[5].cons_waits(), inc=1, tag=f"t{t} wfm32 i")
            v2i = v_op(lambda v: v.tensor_copy(WFMB[0:8, TL:2 * TL],
                                               PS[5][0:8, 0:512]),
                       inc=1, tag=f"t{t} wfmb i")
            B[5].add_cons("V", v2i)
            for kc in range(4):
                bank_w = B[4 if kc % 2 == 0 else 5]
                mm_group(bank_w, [(ESEL[0:8, kc * 128:(kc + 1) * 128],
                                   WFM32[0:8, :])],
                         extra_waits=[("V", v1i), cY],
                         tag=f"t{t} wexpi{kc}")
                vp2 = v_op(lambda v, kc=kc, bank_w=bank_w:
                           v.scalar_tensor_tensor(
                               PRODI[:, kc * TL:(kc + 1) * TL],
                               bank_w.ap[:, :], ALPH[:, 0:1],
                               PTMP[:, kc * TL:(kc + 1) * TL],
                               ALU.mult, ALU.mult),
                           waits=bank_w.cons_waits() + [cY], inc=1,
                           tag=f"t{t} prod2i{kc}")
                bank_w.add_cons("V", vp2)
            w_done_i = p.count["V"]
            slot_load["C"] = g_dma(SLOTC[:, :], scr_ui[:, :],
                                   waits=[("T", p.count["T"]), d_ui],
                                   tag=f"t{t} C<-ui")
            slot_load["D"] = g_dma(SLOTD[:, :], scr_uf[:, :],
                                   waits=[("T", p.count["T"]), d_uf],
                                   tag=f"t{t} D<-uf")

            s_pre = 0
            for fc in range(8):
                bank = B[4 if fc % 2 == 0 else 5]
                pairs = []
                for dc in range(4):
                    pairs.append((SLOTD[:, dc * D + fc * 128:
                                        dc * D + (fc + 1) * 128],
                                  PRODF[:, dc * TL:(dc + 1) * TL]))
                pairs.append((BFT[0:8, fc * 128:(fc + 1) * 128],
                              WFMB[0:8, 0:TL]))
                for dc in range(4):
                    pairs.append((SLOTC[:, dc * D + fc * 128:
                                        dc * D + (fc + 1) * 128],
                                  PRODI[:, dc * TL:(dc + 1) * TL]))
                pairs.append((BIT[0:8, fc * 128:(fc + 1) * 128],
                              WFMB[0:8, TL:2 * TL]))
                xw = [slot_load["D"], slot_load["C"],
                      ("V", w_done_i), d_bf, ("V", v_bi)]
                mm_group(bank, pairs, extra_waits=xw, tag=f"t{t} pre{fc}")
                s_pre = s_op(lambda s, fc=fc, bank=bank: s.copy(
                    PREB[:, fc * TL:(fc + 1) * TL], bank.ap[:, :]),
                    waits=bank.cons_waits(), inc=1, tag=f"t{t} pre{fc} ev")
                bank.add_cons("S", s_pre)

            xbl = [XBLK0, XBLK1]
            for blk in range(4):
                dxb = y_dma(xbl[blk % 2][:, :],
                            x_in[t * TL + blk * 128:
                                 t * TL + (blk + 1) * 128, :],
                            waits=[("V", xb_guard[blk % 2])],
                            tag=f"t{t} xres{blk}",
                            ring="Y1" if blk % 2 == 0 else "Y3")
                evv2 = 0
                dovs = []
                for half in range(2):
                    bank = B[[4, 5, 0, 1][(blk * 2 + half) % 4]]
                    pairs = []
                    for fc in range(8):
                        pairs.append((PREB[:, fc * TL + blk * 128:
                                           fc * TL + (blk + 1) * 128],
                                      qslot[:, fc * D + half * 512:
                                            fc * D + (half + 1) * 512]))
                    pairs.append((OATT[:, t * TL + blk * 128:
                                       t * TL + (blk + 1) * 128],
                                  WOUT[:, half * 512:(half + 1) * 512]))
                    xw = [("S", s_pre), slot_load[qphys],
                          d_wout, ("S", exch["s_oatt"])]
                    mm_group(bank, pairs, extra_waits=xw,
                             tag=f"t{t} out{blk}.{half}")
                    ob = OUTB0 if blk % 2 == 0 else OUTB1
                    evv2 = v_op(lambda v, ob=ob, half=half, bank=bank,
                                xb=xbl[blk % 2]:
                                v.tensor_add(
                                    ob[:, half * 512:(half + 1) * 512],
                                    bank.ap[:, :],
                                    xb[:, half * 512:(half + 1) * 512]),
                                waits=bank.cons_waits() + [dxb] +
                                      [out_guard[blk % 2]],
                                inc=1, tag=f"t{t} outev{blk}.{half}")
                    bank.add_cons("V", evv2)
                    dovs.append(y_dma(
                        out_d[t * TL + blk * 128:t * TL + (blk + 1) * 128,
                              half * 512:(half + 1) * 512],
                        (OUTB0 if blk % 2 == 0 else OUTB1)[
                            :, half * 512:(half + 1) * 512],
                        waits=[("V", evv2)], tag=f"t{t} outdma{blk}.{half}",
                        ring="Y2" if half == 0 else "Y0"))
                xb_guard[blk % 2] = evv2
                out_guard[blk % 2] = dovs
            if t < NT - 1:
                slot_load[qphys] = g_dma(qslot[:, :], scr_wk[:, :],
                                         waits=[("T", p.count["T"]), d_wk],
                                         tag=f"t{t} {qphys}<-wk'")
                slot_load["C"] = g_dma(SLOTC[:, :], scr_vf[:, :],
                                       waits=[("T", p.count["T"]), d_vf],
                                       tag=f"t{t} C<-vf'")
                slot_load["D"] = g_dma(SLOTD[:, :], scr_wf[:, :],
                                       waits=[("T", p.count["T"]), d_wf],
                                       tag=f"t{t} D<-wf'")

            if debug and t == 0:
                y_dma(dbg["qfm"], QFM[:, :], waits=[("S", s_qk)], tag="dbg q")
                y_dma(dbg["kfm"], KFM[:, :], waits=[("S", s_qk)], tag="dbg k")
                y_dma(dbg["cumf"], CUMF[:, :],
                      waits=[("V", scan_mark["f"][3])], tag="dbg cumf")
                y_dma(dbg["cumi"], CUMI[:, :],
                      waits=[("V", scan_mark["i"][3])], tag="dbg cumi")
                y_dma(dbg["wfm"], WFMB[0:8, :], waits=[("V", v2i)],
                      tag="dbg wfm")
                y_dma(dbg["preb"], PREB[:, :], waits=[("S", s_pre)],
                      tag="dbg preb")
                y_dma(dbg["initf"], SML[:, 16:32], waits=[("V", v_initi)],
                      tag="dbg initf")

        p.emit("G", lambda g: g.nop(),
               waits=[(s, p.count[s]) for s in
                      ("Y0", "Y1", "Y2", "Y3", "G0", "G1", "G2", "G3")],
               tag="final wait")

        block = st.enter_context(nc.Block())

        def make_runner(eng_key):
            def run(eng):
                for idx, (waits, fn, inc, sem, tag) in enumerate(p.ops[eng_key]):
                    for sk, val in waits:
                        eng.wait_ge(sems[sk], val)
                    inst = fn(eng)
                    drained = (eng_key == "V" or
                               (eng_key == "S" and
                                (idx < S_DRAIN_LIMIT if S_DRAIN_SET is None
                                 else idx in S_DRAIN_SET)))
                    if drained:
                        d = eng.drain()
                        if inc:
                            d.then_inc(sems[sem], inc)
                    elif inc:
                        inst.then_inc(sems[sem], inc)
            return run

        block.tensor(make_runner("T"))
        block.vector(make_runner("V"))
        block.scalar(make_runner("S"))
        block.gpsimd(make_runner("G"))
        block.sync(make_runner("Y"))

    return nc


def _img8(w):
    c = w.shape[0] // 128
    return np.ascontiguousarray(
        w.reshape(c, 128, -1).transpose(1, 0, 2).reshape(128, -1))


def _madd():
    m = np.full((128, 128), -1e30, np.float32)
    for h in range(8):
        m[h * 16:(h + 1) * 16, h * 16:(h + 1) * 16] = 0.0
    return m


def _esel():
    e = np.zeros((8, 512), np.float32)
    for kc in range(4):
        for pp in range(128):
            e[2 * kc + pp // 64, kc * 128 + pp] = 1.0
    return e


_cache = {}


def _get_nc(debug=False):
    key = ("nc", debug, S_DRAIN_LIMIT, tuple(sorted(S_DRAIN_SET)) if S_DRAIN_SET else None)
    if key not in _cache:
        _cache[key] = build_nc(debug)
    return _cache[key]


def make_in_maps(inputs):
    f32 = np.float32
    x = np.asarray(inputs["L0_attn_x"], f32).reshape(T_FULL, D)
    gamma = np.asarray(inputs["L0_attn_norm_weight"], f32)
    w_qkv = np.asarray(inputs["L0_attn_w_qkv"], f32)
    temp = np.asarray(inputs["L0_attn_temperature"], f32).reshape(8)
    w_out = np.asarray(inputs["L0_attn_w_out"], f32)
    W_Q = np.asarray(inputs["L0_ffn_W_Q"], f32)
    W_K = np.asarray(inputs["L0_ffn_W_K"], f32)
    W_O = np.asarray(inputs["L0_ffn_W_O"], f32)
    W_inv = np.asarray(inputs["L0_ffn_W_inv"], f32)
    V_f = np.asarray(inputs["L0_ffn_V_fwd"], f32)
    W_f = np.asarray(inputs["L0_ffn_W_fwd"], f32)
    U_f = np.asarray(inputs["L0_ffn_U_fwd"], f32)
    b_f = np.asarray(inputs["L0_ffn_b_fwd"], f32)
    V_i = np.asarray(inputs["L0_ffn_V_inv"], f32)
    WIE = np.asarray(inputs["L0_ffn_W_inv_exp"], f32)
    U_i = np.asarray(inputs["L0_ffn_U_inv"], f32)
    b_i = np.asarray(inputs["L0_ffn_b_inv"], f32)
    rw1 = np.asarray(inputs["L0_ffn_router_w1"], f32)
    rb1 = np.asarray(inputs["L0_ffn_router_b1"], f32)
    rw2 = np.asarray(inputs["L0_ffn_router_w2"], f32)
    rb2 = np.asarray(inputs["L0_ffn_router_b2"], f32)
    alpha = np.asarray(inputs["L0_ffn_alpha_bi"], f32).reshape(())
    ebias = np.asarray(inputs["L0_ffn_expert_bias"], f32)

    shared = {
        "wq_im": _img8(np.ascontiguousarray(W_Q.T)),
        "wk_im": _img8(np.ascontiguousarray(W_K.T)),
        "wo_im": _img8(np.ascontiguousarray(W_O.T)),
        "rw1_im": _img8(np.ascontiguousarray(rw1.T)),
        "winv_im": _img8(W_inv),
        "vf_im": _img8(V_f.transpose(1, 0, 2).reshape(D, KR)),
        "wf_im": _img8(W_f.transpose(1, 0, 2).reshape(D, KR)),
        "uf_im": _img8(U_f.transpose(0, 2, 1).reshape(KR, D)),
        "wie_im": _img8(WIE.transpose(1, 0, 2).reshape(D, KR)),
        "vi_im": _img8(V_i.transpose(1, 0, 2).reshape(D, KR)),
        "ui_im": _img8(U_i.transpose(0, 2, 1).reshape(KR, D)),
        "wqkv_im": _img8(np.ascontiguousarray(w_qkv.T)),
        "wout_im": np.ascontiguousarray(w_out.T),
        "bf_im": np.ascontiguousarray(b_f),
        "bi_im": np.ascontiguousarray(b_i),
        "rw2_im": _img8(np.ascontiguousarray(rw2.T)),
        "gam": np.ascontiguousarray(gamma.reshape(8, 128).T),
        "b1g": np.ascontiguousarray(rb1.reshape(8, 128).T),
        "tmpr": np.ascontiguousarray(np.repeat(temp, 16).reshape(128, 1)),
        "alph": np.ascontiguousarray(
            np.broadcast_to(alpha, (128, 1)).astype(f32)),
        "b2c": np.ascontiguousarray(rb2.reshape(8, 1)),
        "ebc": np.ascontiguousarray(ebias.reshape(8, 1)),
        "madd": _madd(),
        "idf": np.eye(128, dtype=f32),
        "idb": np.eye(128, dtype=BF),
        "onef": np.ones((128, 512), f32),
        "onesb": np.ones((128, 512), BF),
        "esel": _esel(),
    }
    in_maps = []
    for c in range(N_CORES):
        m = dict(shared)
        m["x"] = np.ascontiguousarray(x[c * L:(c + 1) * L])
        pm = np.zeros((128, 8), f32)
        pm[:, :c] = 1.0
        m["pmsk"] = pm
        gidx = c * L + np.arange(L).reshape(NB, 128)
        m["rcp"] = np.ascontiguousarray((1.0 / (gidx + 1)).astype(f32).T)
        in_maps.append(m)
    return in_maps


def kernel(**inputs):
    nc = _get_nc(DEBUG)
    in_maps = make_in_maps(inputs)
    res = run_bass_kernel_spmd(nc, in_maps, core_ids=list(range(N_CORES)))
    out = np.concatenate([res.results[c]["out"] for c in range(N_CORES)],
                         axis=0)
    if DEBUG:
        kernel.last_results = res
    return out.reshape(1, T_FULL, D).astype(np.float32)


# version 16
# speedup vs baseline: 1.0693x; 1.0693x over previous
import contextlib
import numpy as np
import ml_dtypes

import concourse.bass as bass
import concourse.mybir as mybir
from concourse.bass_utils import run_bass_kernel_spmd

F32 = mybir.dt.float32
BF16 = mybir.dt.bfloat16
AF = mybir.ActivationFunctionType
ALU = mybir.AluOpType
AX = mybir.AxisListType
BF = ml_dtypes.bfloat16

N_CORES = 8
D = 1024
T_FULL = 16384
L = T_FULL // N_CORES
NB = L // 128
TL = 512
NT = L // TL
KR = 512

DEBUG = False
S_DRAIN_LIMIT = 10**9
S_DRAIN_SET = None


class Plan:
    ENGINES = ("T", "V", "S", "G", "Y")

    def __init__(self):
        self.ops = {e: [] for e in self.ENGINES}
        self.count = {e: 0 for e in self.ENGINES}
        for s in ("CC", "Y0", "Y1", "Y2", "Y3", "G0", "G1", "G2", "G3"):
            self.count[s] = 0

    def emit(self, eng, fn, waits=(), inc=0, inc_sem=None, tag=""):
        flat = []
        for w in waits:
            if isinstance(w, list):
                flat += w
            else:
                flat.append(w)
        waits = [(s, v) for (s, v) in flat if v > 0]
        for sem, val in waits:
            assert val <= self.count[sem], (
                f"wait on future {sem}={val} (now {self.count[sem]}) at {tag}")
        sem = inc_sem or eng
        self.ops[eng].append((waits, fn, inc, sem, tag))
        if inc:
            self.count[sem] += inc
        return self.count[sem]


class Bank:

    def __init__(self, ap):
        self.ap = ap
        self.prod = ("T", 0)
        self.cons = []

    def prod_waits(self):
        w, self.cons = self.cons, []
        return w

    def set_prod(self, val):
        self.prod = ("T", val)

    def cons_waits(self):
        return [self.prod]

    def add_cons(self, eng, val):
        self.cons.append((eng, val))


def build_nc(debug=False):
    nc = bass.Bass("TRN2", target_bir_lowering=False)

    def din(name, shape, dt=F32):
        return nc.dram_tensor(name, shape, dt, kind="ExternalInput").ap()

    x_in = din("x", [L, D])
    wq_im = din("wq_im", [128, 8 * D])
    wk_im = din("wk_im", [128, 8 * D])
    wo_im = din("wo_im", [128, 8 * D])
    rw1_im = din("rw1_im", [128, 8 * D])
    winv_im = din("winv_im", [128, 8 * D])
    vf_im = din("vf_im", [128, 8 * KR])
    wf_im = din("wf_im", [128, 8 * KR])
    uf_im = din("uf_im", [128, 4 * D])
    wie_im = din("wie_im", [128, 8 * KR])
    vi_im = din("vi_im", [128, 8 * KR])
    ui_im = din("ui_im", [128, 4 * D])
    wqkv_im = din("wqkv_im", [128, 8 * 384])
    wout_im = din("wout_im", [128, D])
    bf_im = din("bf_im", [8, D])
    bi_im = din("bi_im", [8, D])
    rw2_im = din("rw2_im", [128, 64])
    gam_in = din("gam", [128, 8])
    b1g_in = din("b1g", [128, 8])
    tmpr_in = din("tmpr", [128, 1])
    alph_in = din("alph", [128, 1])
    b2_in = din("b2c", [8, 1])
    eb_in = din("ebc", [8, 1])
    pmsk_in = din("pmsk", [128, 8])
    rcp_in = din("rcp", [128, 16])
    madd_in = din("madd", [128, 128])
    idf_in = din("idf", [128, 128])
    idb_in = din("idb", [128, 128], BF16)
    onef_in = din("onef", [128, 512])
    onesb_in = din("onesb", [128, 512], BF16)
    esel_in = din("esel", [8, 512])

    out_d = nc.dram_tensor("out", [L, D], F32, kind="ExternalOutput").ap()
    dbg = {}
    if debug:
        def dout(name, shape, dt=F32):
            dbg[name] = nc.dram_tensor("dbg_" + name, shape, dt,
                                       kind="ExternalOutput").ap()
        dout("xnfm", [128, 8 * L], BF16)
        dout("gt", [128, 144])
        dout("att", [128, 128])
        dout("oatt", [128, L], BF16)
        dout("prf", [128, 8])
        dout("m1", [128, 8 * KR], BF16)
        dout("m2", [128, 8 * D], BF16)
        dout("qfm", [128, 8 * TL], BF16)
        dout("kfm", [128, 8 * TL], BF16)
        dout("cumf", [128, 4 * TL])
        dout("cumi", [128, 4 * TL])
        dout("wfm", [8, 2 * TL], BF16)
        dout("preb", [128, 8 * TL], BF16)
        dout("initf", [128, 16])
        dout("grd", [128, 192])
        dout("xnt", [128, 2048], BF16)

    scr_wk = nc.dram_tensor("scr_wk", [128, 8 * D], BF16).ap()
    scr_rw1 = nc.dram_tensor("scr_rw1", [128, 8 * D], BF16).ap()
    scr_wo = nc.dram_tensor("scr_wo", [128, 8 * D], BF16).ap()
    scr_m2 = nc.dram_tensor("scr_m2", [128, 8 * D], BF16).ap()
    scr_vf = nc.dram_tensor("scr_vf", [128, 8 * KR], BF16).ap()
    scr_wf = nc.dram_tensor("scr_wf", [128, 8 * KR], BF16).ap()
    scr_uf = nc.dram_tensor("scr_uf", [128, 4 * D], BF16).ap()
    scr_ui = nc.dram_tensor("scr_ui", [128, 4 * D], BF16).ap()
    scr_m1 = nc.dram_tensor("scr_m1", [128, 8 * KR], BF16).ap()
    scr_m3 = nc.dram_tensor("scr_m3", [128, 8 * KR], BF16).ap()
    ccg_in = nc.dram_tensor("ccg_in", [128, 144], F32).ap()
    ccg_out = nc.dram_tensor("ccg_out", [8, 128, 144], F32).ap()

    st = contextlib.ExitStack()
    with st:
        def sbuf(name, shape, dt=F32):
            return st.enter_context(nc.sbuf_tensor(name, shape, dt))

        CUMF = sbuf("CUMF", [128, 4 * TL])
        CUMI = sbuf("CUMI", [128, 4 * TL])
        XBLK0 = sbuf("XBLK0", [128, D])
        XBLK1 = sbuf("XBLK1", [128, D])
        OUTB0 = sbuf("OUTB0", [128, D])
        OUTB1 = sbuf("OUTB1", [128, D])
        GRD = sbuf("GRD", [128, 192])
        SML = sbuf("SML", [128, 160])
        STG = sbuf("STG", [128, 144])
        AGR = sbuf("AGR", [128, 8 * 144])
        GT = sbuf("GT", [128, 144])
        ATT = sbuf("ATT", [128, 128])
        LG = sbuf("LG", [128, 512])
        WFM32 = sbuf("WFM32", [8, 512])
        WTM = sbuf("WTM", [128, 56])
        ATT2 = sbuf("ATT2", [128, 128])
        PTMP = sbuf("PTMP", [128, 4 * 512], BF16)
        PRF = sbuf("PRF", [128, 8])
        BRW = sbuf("BRW", [8, 1])
        ONES = sbuf("ONES", [128, 512])
        IDF = sbuf("IDF", [128, 128])
        MADD = sbuf("MADD", [128, 128])
        GAM = sbuf("GAM", [128, 8])
        B1G = sbuf("B1G", [128, 8])
        TMPR = sbuf("TMPR", [128, 1])
        ALPH = sbuf("ALPH", [128, 1])
        PMSK = sbuf("PMSK", [128, 8])
        RCP = sbuf("RCP", [128, 16])
        ESEL = sbuf("ESEL", [8, 512])

        XNFM = sbuf("XNFM", [128, 8 * L], BF16)
        VFM = sbuf("VFM", [128, L], BF16)
        OATT = sbuf("OATT", [128, L], BF16)
        SLOTA = sbuf("SLOTA", [128, 8 * D], BF16)
        SLOTB = sbuf("SLOTB", [128, 8 * D], BF16)
        SLOTC = sbuf("SLOTC", [128, 8 * KR], BF16)
        SLOTD = sbuf("SLOTD", [128, 8 * KR], BF16)
        SLOTE = sbuf("SLOTE", [128, 8 * D], BF16)
        QFM = sbuf("QFM", [128, 8 * TL], BF16)
        KFM = sbuf("KFM", [128, 8 * TL], BF16)
        PRODF = sbuf("PRODF", [128, 4 * TL], BF16)
        PRODI = sbuf("PRODI", [128, 4 * TL], BF16)
        HCH = sbuf("HCH", [128, 1024], BF16)
        PREB = sbuf("PREB", [128, 8 * TL], BF16)
        WQKV = sbuf("WQKV", [128, 8 * 384], BF16)
        WOUT = sbuf("WOUT", [128, D], BF16)
        BFT = sbuf("BFT", [8, D], BF16)
        BIT = sbuf("BIT", [8, D], BF16)
        RW2B = sbuf("RW2B", [128, 64], BF16)
        WFMB = sbuf("WFMB", [8, 2 * TL], BF16)
        XNT = QFM[:, 0:2048]
        PRFB = sbuf("PRFB", [128, 8], BF16)
        ATB = sbuf("ATB", [128, 128], BF16)
        IDB = sbuf("IDB", [128, 128], BF16)
        ONESB = sbuf("ONESB", [128, 512], BF16)

        PS = [st.enter_context(nc.psum_tensor(f"ps{i}", [128, 512], F32))
              for i in range(6)]
        PSB = [st.enter_context(nc.psum_tensor(f"psb{i}", [128, 1024], BF16))
               for i in range(2)]

        sems = {s: st.enter_context(nc.semaphore(f"sem_{s}"))
                for s in ("T", "V", "S", "CC",
                          "Y0", "Y1", "Y2", "Y3", "G0", "G1", "G2", "G3")}

        p = Plan()
        B = [Bank(ps) for ps in PS]
        BB = [Bank(ps) for ps in PSB]

        def t_mm(out_ap, lhsT, rhs, start, stop, waits=(), inc=0, tag=""):
            return p.emit(
                "T",
                lambda t, o=out_ap, a=lhsT, b=rhs, s0=start, s1=stop:
                    t.matmul(o, a, b, start=s0, stop=s1, skip_group_check=True),
                waits, inc, tag=tag)

        def t_tp(out_ap, in_ap, ident, waits=(), inc=0, tag=""):
            return p.emit(
                "T",
                lambda t, o=out_ap, i=in_ap, d=ident: t.transpose(o, i, d),
                waits, inc, tag=tag)

        def v_op(fn, waits=(), inc=0, tag=""):
            return p.emit("V", fn, waits, inc, tag=tag)

        def s_op(fn, waits=(), inc=0, tag=""):
            return p.emit("S", fn, waits, inc, tag=tag)

        def _dma(eng, ring, out_ap, in_ap, waits, tag, chain=True):
            w = list(waits)
            prev = p.count[ring]
            if chain and prev:
                w.append((ring, prev))
            val = p.emit(
                eng,
                lambda e, o=out_ap, i=in_ap: e.dma_start(out=o, in_=i),
                w, 16, inc_sem=ring, tag=tag)
            return (ring, val)

        _slot_ring = [0]

        def g_dma(out_ap, in_ap, waits=(), tag="", ring=None):
            if ring is None:
                if "<-" in tag:
                    ring = "G1" if _slot_ring[0] % 2 == 0 else "G2"
                    _slot_ring[0] += 1
                else:
                    ring = "G0"
            return _dma("G", ring, out_ap, in_ap, waits, tag)

        def g_dma2(out_ap, in_ap, waits=(), tag=""):
            half = out_ap.shape[-1] // 2
            t1 = _dma("G", "G1", out_ap[:, 0:half], in_ap[:, 0:half],
                      waits, tag + " lo")
            t2 = _dma("G", "G2", out_ap[:, half:], in_ap[:, half:],
                      waits, tag + " hi")
            return [t1, t2]

        def y_dma(out_ap, in_ap, waits=(), tag="", ring="Y0", chain=True):
            return _dma("Y", ring, out_ap, in_ap, waits, tag, chain=chain)

        def mm_group(bank, pairs, out_ap=None, extra_waits=(), tag=""):
            o = out_ap if out_ap is not None else bank.ap[:, :]
            n = len(pairs)
            tv = 0
            for i, (a, b) in enumerate(pairs):
                w = (list(extra_waits) + bank.prod_waits()) if i == 0 else []
                tv = t_mm(o, a, b, i == 0, i == n - 1, waits=w,
                          inc=1 if i == n - 1 else 0, tag=f"{tag}.{i}")
            bank.set_prod(tv)
            return tv

        d_winv_a = g_dma(SLOTA[:, 0:4 * D], winv_im[:, 0:4 * D],
                         ring="G0", tag="winv->A lo")
        d_winv_b = g_dma(SLOTA[:, 4 * D:8 * D], winv_im[:, 4 * D:8 * D],
                         ring="G3", tag="winv->A hi")
        d_wie = g_dma(SLOTC[:, :], wie_im[:, :], ring="G2", tag="wie->C")
        d_rw1s = g_dma(SLOTB[:, :], rw1_im[:, :], ring="G1",
                       waits=[d_wie], tag="rw1->B")
        d_vi = g_dma(SLOTD[:, :], vi_im[:, :], ring="G0", tag="vi->D")
        wqkv_dd = []
        for cc in range(8):
            wqkv_dd.append(g_dma(
                PREB[:, cc * 384:(cc + 1) * 384],
                wqkv_im[:, cc * 384:(cc + 1) * 384],
                tag=f"wqkv stage {cc}", ring="G0" if cc % 2 == 0 else "G3"))

        cY = 0
        for dst, src in ((ONES, onef_in), (IDF, idf_in), (MADD, madd_in),
                         (GAM, gam_in), (B1G, b1g_in), (TMPR, tmpr_in),
                         (ALPH, alph_in), (PMSK, pmsk_in), (RCP, rcp_in),
                         (IDB, idb_in), (ONESB, onesb_in), (ESEL, esel_in)):
            cY = y_dma(dst[:, :], src[:, :], tag="const", ring="Y2",
                       chain=False)
        d_b2 = y_dma(SML[0:8, 0:1], b2_in[:, :], tag="b2")
        d_eb = y_dma(SML[0:8, 1:2], eb_in[:, :], tag="eb")
        v_brw = v_op(lambda v: v.tensor_add(BRW[0:8, 0:1], SML[0:8, 0:1],
                                            SML[0:8, 1:2]),
                     waits=[d_eb], inc=1, tag="brw")

        d_bi = y_dma(XBLK0[0:8, 0:D], bi_im[:, :], tag="bi stage")
        v_bi = v_op(lambda v: v.tensor_scalar(BIT[0:8, :], XBLK0[0:8, 0:D],
                                              ALPH[0:8, 0:1], None, ALU.mult),
                    waits=[d_bi, cY], inc=1, tag="bi*alpha")
        fam_w = {"m1": [d_winv_a, d_winv_b, d_wie],
                 "m3": [d_winv_a, d_winv_b, d_vi],
                 "m2": [d_winv_a, d_winv_b, d_rw1s]}
        dump_hist = []
        npre = 0
        for name, rhs_t, scr, nhalf in (("m1", SLOTC, scr_m1, 1),
                                        ("m3", SLOTD, scr_m3, 1),
                                        ("m2", SLOTB, scr_m2, 2)):
            for jc in range(8):
                for half in range(nhalf):
                    bank = B[4 + (npre % 2)]
                    hreg = HCH[:, (npre % 2) * 512:(npre % 2 + 1) * 512]
                    pairs = [(SLOTA[:, ic * D + jc * 128:
                                    ic * D + (jc + 1) * 128],
                              rhs_t[:, ic * (512 * nhalf) + half * 512:
                                    ic * (512 * nhalf) + (half + 1) * 512])
                             for ic in range(8)]
                    tv = mm_group(bank, pairs, extra_waits=fam_w[name],
                                  tag=f"{name} {jc}.{half}")
                    wv = bank.cons_waits()
                    if len(dump_hist) >= 2:
                        wv = wv + [dump_hist[-2]]
                    ev = s_op(lambda s, bank=bank, hreg=hreg:
                              s.copy(hreg, bank.ap[:, :]),
                              waits=wv, inc=1, tag=f"{name} ev {jc}.{half}")
                    bank.add_cons("S", ev)
                    dv = g_dma(scr[:, jc * 512 * nhalf + half * 512:
                                   jc * 512 * nhalf + (half + 1) * 512],
                               hreg, waits=[("S", ev)], ring="G3",
                               tag=f"{name} dump {jc}.{half}")
                    dump_hist.append(dv)
                    npre += 1
        t_m_done = p.count["T"]
        g_m_done = ("G3", p.count["G3"])

        xbufs = [OUTB0, OUTB1]
        buf_cons = {0: (0, 0), 1: (0, 0)}
        last_sqred = 0
        for b in range(NB):
            buf = xbufs[b % 2]
            pv, ps_ = buf_cons[b % 2]
            dx = y_dma(buf[:, :], x_in[b * 128:(b + 1) * 128, :],
                       waits=[("V", pv), ("S", ps_)], tag=f"x1 {b}",
                       ring="Y1" if b % 2 == 0 else "Y3")
            vm = v_op(lambda v, b=b, buf=buf: v.reduce_sum(
                GRD[:, b:b + 1], buf[:, :], axis=AX.X),
                waits=[dx], inc=1, tag=f"msum {b}")
            v_op(lambda v, buf=buf: v.tensor_mul(
                CUMF[:, 0:D], buf[:, :], buf[:, :]), tag=f"sq {b}")
            last_sqred = v_op(lambda v, b=b: v.reduce_sum(
                GRD[:, 16 + b:17 + b], CUMF[:, 0:D], axis=AX.X),
                inc=1, tag=f"sqred {b}")
            buf_cons[b % 2] = (last_sqred, 0)
        v_allstats = last_sqred

        def G_(a, b):
            return GRD[:, a:b]
        v_op(lambda v: v.tensor_scalar(G_(32, 48), G_(0, 16), 1.0 / D, None,
                                       ALU.mult),
             waits=[("V", v_allstats)], tag="m")
        v_op(lambda v: v.tensor_scalar(G_(128, 144), G_(16, 32), 1.0 / D,
                                       None, ALU.mult), tag="Ex2")
        v_op(lambda v: v.tensor_mul(G_(144, 160), G_(32, 48), G_(32, 48)),
             tag="m2")
        v_op(lambda v: v.tensor_sub(G_(48, 64), G_(128, 144), G_(144, 160)),
             tag="var")
        v_op(lambda v: v.tensor_scalar(G_(160, 176), G_(48, 64), 1e-5, None,
                                       ALU.add), tag="v+eps")
        v_rec = v_op(lambda v: v.reciprocal(G_(176, 192), G_(160, 176)),
                     inc=1, tag="recip")
        s_s = s_op(lambda s: s.activation(G_(64, 80), G_(176, 192), AF.Sqrt),
                   waits=[("V", v_rec)], inc=1, tag="s=sqrt")
        v_op(lambda v: v.tensor_mul(G_(128, 144), G_(32, 48), G_(64, 80)),
             waits=[("S", s_s)], tag="ms")
        v_negms = v_op(lambda v: v.tensor_scalar(G_(80, 96), G_(128, 144),
                                                 -1.0, None, ALU.mult),
                       inc=1, tag="negms")
        v_op(lambda v: v.tensor_mul(G_(144, 160), G_(48, 64), G_(176, 192)),
             tag="ms2")
        v_op(lambda v: v.tensor_scalar(G_(160, 176), G_(144, 160), 1e-6,
                                       None, ALU.add), tag="+eps2")
        v_r = v_op(lambda v: v.reciprocal(G_(128, 144), G_(160, 176)),
                   inc=1, tag="rr")
        s_r = s_op(lambda s: s.activation(G_(96, 112), G_(128, 144), AF.Sqrt),
                   waits=[("V", v_r)], inc=1, tag="r=sqrt")
        v_r2 = s_r

        v_qkv = 0
        for cc in range(8):
            v_qkv = v_op(lambda v, cc=cc: v.tensor_scalar(
                WQKV[:, cc * 384:(cc + 1) * 384],
                PREB[:, cc * 384:(cc + 1) * 384],
                GAM[:, cc:cc + 1], None, ALU.mult),
                waits=[wqkv_dd[cc], cY], inc=1, tag=f"wqkv*g {cc}")

        ACC = B[3]
        xb2 = [XBLK0, XBLK1, OUTB0]
        last_x2c = {0: v_bi, 1: 0, 2: v_allstats}
        qkv_banks = [B[0], B[1]]
        xnfm_ev = {}
        xn_guard = {0: 0, 1: 0}
        vtp_guard = 0
        s_vfm = 0
        x2_tok = {}
        for b in range(NB):
            buf = xb2[b % 3]
            dx = y_dma(buf[:, :], x_in[b * 128:(b + 1) * 128, :],
                       waits=[("V", last_x2c[b % 3])], tag=f"x2 {b}",
                       ring="Y1" if b % 2 == 0 else "Y3")
            x2_tok[b] = dx
            xn_t = XNT[:, (b % 2) * 1024:(b % 2 + 1) * 1024]
            w = [dx, ("V", v_negms), ("S", s_r),
                 ("T", xn_guard[b % 2])]
            s_xn = s_op(lambda s, buf=buf, xt=xn_t, b=b: s.activation(
                xt, buf[:, :], AF.Identity, bias=GRD[:, 80 + b:81 + b],
                scale=GRD[:, 64 + b:65 + b]),
                waits=w, inc=1, tag=f"xn {b}")
            ev = 0
            for rnd in range(2):
                tvs = 0
                for i in range(4):
                    c = rnd * 4 + i
                    w = ([("S", s_xn), cY] + BB[0].prod_waits()
                         if i == 0 else [])
                    tvs = t_tp(PSB[0][:, c * 128:(c + 1) * 128],
                               XNT[:, (b % 2) * 1024 + c * 128:
                                   (b % 2) * 1024 + (c + 1) * 128],
                               IDB[:, :], waits=w,
                               inc=1 if i == 3 else 0, tag=f"tp {b}.{c}")
                BB[0].set_prod(tvs)
                for i in range(4):
                    c = rnd * 4 + i
                    ev = v_op(lambda v, b=b, c=c: v.tensor_copy(
                        XNFM[:, c * L + b * 128:c * L + (b + 1) * 128],
                        PSB[0][:, c * 128:(c + 1) * 128]),
                        waits=BB[0].cons_waits() if i == 0 else [],
                        inc=1, tag=f"tpev {b}.{c}")
                BB[0].add_cons("V", ev)
            xnfm_ev[b] = ev
            tsx = 0
            for c in range(8):
                tsx = t_mm(ACC.ap[:, 136 + c:137 + c],
                           XNT[:, (b % 2) * 1024 + c * 128:
                               (b % 2) * 1024 + (c + 1) * 128],
                           ONESB[:, 0:1], b == 0, b == NB - 1,
                           waits=([("S", s_xn)] + ([cY] if b == 0 else []))
                           if c == 0 else [],
                           inc=1 if c == 7 else 0, tag=f"sumxn {b}.{c}")
            qb = qkv_banks[b % 2]
            pairs = [(XNFM[:, c * L + b * 128:c * L + (b + 1) * 128],
                      WQKV[:, c * 384:(c + 1) * 384]) for c in range(8)]
            tq = mm_group(qb, pairs, out_ap=qb.ap[:, 0:384],
                          extra_waits=[("V", xnfm_ev[b]), ("V", v_qkv)],
                          tag=f"qkv {b}")
            xn_guard[b % 2] = tsx
            sq = s_op(lambda s, qb=qb, b=b: s.activation(
                PRODF[:, 0:128], qb.ap[:, 0:128], AF.Copy,
                scale=GRD[:, 96 + b:97 + b]),
                waits=qb.cons_waits() + [("T", vtp_guard)],
                inc=1, tag=f"qtm {b}")
            sk = s_op(lambda s, qb=qb, b=b: s.activation(
                PRODF[:, 128:256], qb.ap[:, 128:256], AF.Copy,
                scale=GRD[:, 96 + b:97 + b]), inc=1, tag=f"ktm {b}")
            sv = s_op(lambda s, qb=qb, b=b: s.activation(
                PRODF[:, 256:384], qb.ap[:, 256:384], AF.Copy,
                scale=GRD[:, 96 + b:97 + b]), inc=1, tag=f"vtm {b}")
            vq2 = v_op(lambda v, qb=qb, b=b: v.scalar_tensor_tensor(
                CUMI[:, 0:128], qb.ap[:, 0:128], GRD[:, 96 + b:97 + b],
                PRODF[:, 0:128], ALU.mult, ALU.mult),
                waits=qb.cons_waits() + [("S", s_r), ("T", vtp_guard),
                                         ("S", sv)],
                inc=1, tag=f"q2 {b}")
            vk2 = v_op(lambda v, qb=qb, b=b: v.scalar_tensor_tensor(
                CUMI[:, 128:256], qb.ap[:, 128:256], GRD[:, 96 + b:97 + b],
                PRODF[:, 128:256], ALU.mult, ALU.mult),
                waits=[("S", sk)],
                inc=1, tag=f"k2 {b}")
            qb.add_cons("S", sv)
            qb.add_cons("V", vk2)
            t_mm(ACC.ap[:, 0:128], PRODF[:, 0:128], PRODF[:, 128:256],
                 b == 0, b == NB - 1, waits=[("S", sk)], tag=f"gram {b}")
            t_mm(ACC.ap[:, 128:129], CUMI[:, 0:128], ONES[:, 0:1],
                 b == 0, b == NB - 1,
                 waits=[("V", vq2), cY], tag=f"ssq_q {b}")
            t_mm(ACC.ap[:, 129:130], CUMI[:, 128:256], ONES[:, 0:1],
                 b == 0, b == NB - 1, waits=[("V", vk2)], tag=f"ssq_k {b}")
            vtp_guard = t_tp(PSB[1][:, 0:128], PRODF[:, 256:384], IDB[:, :],
                             waits=[("S", sv)] + BB[1].prod_waits(),
                             inc=1, tag=f"vtp {b}")
            BB[1].set_prod(vtp_guard)
            s_vfm = s_op(lambda s, b=b: s.copy(VFM[:, b * 128:(b + 1) * 128],
                                               PSB[1][:, 0:128]),
                         waits=BB[1].cons_waits(), inc=1, tag=f"vfm {b}")
            BB[1].add_cons("S", s_vfm)
            last_x2c[b % 3] = vk2
        ACC.set_prod(p.count["T"])
        xc_guard = dict(last_x2c)

        cast_gate = x2_tok[11]
        d_wk = g_dma(scr_wk[:, :], wk_im[:, :], ring="G2",
                     waits=[cast_gate], tag="cast wk")
        d_vf = g_dma(scr_vf[:, :], vf_im[:, :], ring="G0",
                     waits=[cast_gate], tag="cast vf")
        d_wf = g_dma(scr_wf[:, :], wf_im[:, :], ring="G1", tag="cast wf")
        d_uf = g_dma(scr_uf[:, :], uf_im[:, :], ring="G2", tag="cast uf")
        d_wo = g_dma(scr_wo[:, :], wo_im[:, :], ring="G0", tag="cast wo")
        d_rw2 = g_dma(RW2B[:, :], rw2_im[:, :], ring="G1", tag="rw2 cast")
        d_bf = g_dma(BFT[:, :], bf_im[:, :], ring="G2", tag="bf cast")
        d_wout = g_dma(WOUT[:, :], wout_im[:, :], ring="G1", tag="wout cast")
        d_ui = g_dma(scr_ui[:, :], ui_im[:, :], ring="G2", tag="cast ui")
        d_slote = g_dma(SLOTE[:, :], wq_im[:, :], ring="G1",
                        tag="slote<-wq")
        d_rw1dump = g_dma(scr_rw1[:, :], SLOTB[:, :],
                          waits=[("T", t_m_done)], ring="G3",
                          tag="rw1 dump")

        bigslot = {"A": SLOTA, "B": SLOTB}
        slot_load = {}
        slot_load["A"] = g_dma(SLOTA[:, :], scr_rw1[:, :],
                               waits=[("T", t_m_done), d_rw1dump],
                               tag="A<-rw1")
        slot_load["B"] = g_dma(SLOTB[:, :], scr_wk[:, :],
                               waits=[d_rw1dump, d_wk], tag="B<-wk")
        slot_load["C"] = g_dma(SLOTC[:, :], scr_vf[:, :],
                               waits=[("T", t_m_done), d_vf], tag="C<-vf")
        slot_load["D"] = g_dma(SLOTD[:, :], scr_wf[:, :],
                               waits=[("T", t_m_done), d_wf], tag="D<-wf")

        v_stg = v_op(lambda v: v.tensor_copy(STG[:, 0:144], ACC.ap[:, 0:144]),
                     waits=ACC.cons_waits(), inc=1, tag="stage")
        ACC.add_cons("V", v_stg)
        d_ccin = g_dma(ccg_in[:, :], STG[:, :], waits=[("V", v_stg)],
                       tag="cc in")
        cc1 = p.emit(
            "G",
            lambda g: g.collective_compute(
                "AllGather", ALU.bypass,
                replica_groups=[list(range(N_CORES))],
                ins=[ccg_in[:, :]], outs=[ccg_out[:, :, :]]),
            waits=[d_ccin], inc=1, inc_sem="CC", tag="allgather")
        d_agr = g_dma(AGR.rearrange("p (r c) -> p r c", r=8),
                      ccg_out.rearrange("r p c -> p r c"),
                      waits=[("CC", cc1)], tag="cc back")
        vgt = v_op(lambda v: v.tensor_add(GT[:, 0:144], AGR[:, 0:144],
                                          AGR[:, 144:288]),
                   waits=[d_agr], inc=1, tag="gt01")
        pp = [GT, STG]
        for r in range(2, 8):
            srcb, dstb = pp[(r - 2) % 2], pp[(r - 1) % 2]
            vgt = v_op(lambda v, r=r, srcb=srcb, dstb=dstb: v.tensor_add(
                dstb[:, 0:144], srcb[:, 0:144], AGR[:, r * 144:(r + 1) * 144]),
                inc=1, tag=f"gt{r}")
        vprf = v_op(lambda v: v.tensor_scalar(
            PRF[:, :], AGR[:, 136:144], PMSK[:, 0:1], None, ALU.mult),
            waits=[cY], inc=1, tag="prf0")
        pbufs = [PRF[:, 0:8], SML[:, 40:48]]
        for r in range(1, 8):
            sb_, db_ = pbufs[(r - 1) % 2], pbufs[r % 2]
            vprf = v_op(lambda v, r=r, sb_=sb_, db_=db_: v.scalar_tensor_tensor(
                db_, AGR[:, r * 144 + 136:r * 144 + 144],
                PMSK[:, r:r + 1], sb_, ALU.mult, ALU.add),
                inc=1, tag=f"prf{r}")
        vprf = v_op(lambda v: v.tensor_copy(PRF[:, 0:8], pbufs[1]),
                    inc=1, tag="prf fin")
        vprfb = v_op(lambda v: v.tensor_copy(PRFB[:, :], PRF[:, :]),
                     inc=1, tag="prfb")

        def emit_half(t, which, phys):
            dst = QFM if which == "Q" else KFM
            slot = SLOTE if which == "Q" else bigslot[phys]
            sq_ev = 0
            for oc in range(8):
                bank = B[[0, 1, 4, 5][oc % 4]]
                pairs = [(slot[:, dc * D + oc * 128: dc * D + (oc + 1) * 128],
                          XNFM[:, dc * L + t * TL: dc * L + (t + 1) * TL])
                         for dc in range(8)]
                xw = [d_slote if which == "Q" else slot_load[phys]]
                if t == 0 and which == "Q" and oc == 0:
                    xw.append(("V", xnfm_ev[NB - 1]))
                mm_group(bank, pairs, extra_waits=xw, tag=f"t{t}{which}{oc}")
                sq_ev = s_op(lambda s, dst=dst, oc=oc, bank=bank: s.copy(
                    dst[:, oc * TL:(oc + 1) * TL], bank.ap[:, :]),
                    waits=bank.cons_waits(), inc=1, tag=f"t{t}{which}{oc} ev")
                bank.add_cons("S", sq_ev)
            return sq_ev, p.count["T"]

        _, tq0 = emit_half(0, "Q", "A")
        s_qk0, tk0 = emit_half(0, "K", "B")


        s_nq = s_op(lambda s: s.activation(SML[:, 2:3], GT[:, 128:129],
                                           AF.Sqrt),
                    waits=[("V", vgt)], inc=1, tag="nq")
        s_nk = s_op(lambda s: s.activation(SML[:, 3:4], GT[:, 129:130],
                                           AF.Sqrt), inc=1, tag="nk")
        v_op(lambda v: v.tensor_scalar_max(SML[:, 10:11], SML[:, 2:3], 1e-12),
             waits=[("S", s_nk)], tag="nq max")
        v_op(lambda v: v.tensor_scalar_max(SML[:, 11:12], SML[:, 3:4], 1e-12),
             tag="nk max")
        v_op(lambda v: v.reciprocal(SML[:, 4:5], SML[:, 10:11]), tag="invnq")
        v_nk = v_op(lambda v: v.reciprocal(SML[:, 5:6], SML[:, 11:12]),
                    inc=1, tag="invnk")
        s_et = s_op(lambda s: s.activation(SML[:, 6:7], TMPR[:, 0:1], AF.Exp),
                    waits=[cY], inc=1, tag="exp temp")
        v_rs = v_op(lambda v: v.tensor_mul(SML[:, 7:8], SML[:, 6:7],
                                           SML[:, 4:5]),
                    waits=[("S", s_et)], inc=1, tag="rowscale")

        exch = {"s_oatt": 0, "v_initf": 0}

        def emit_exchange_T():
            t_cs1 = mm_group(B[5], [(SML[:, 5:6], IDF[:, :])],
                             out_ap=PS[5][0:1, 0:128],
                             extra_waits=[("V", v_nk), cY], tag="invnk row")
            v_cs = v_op(lambda v: v.tensor_copy(SML[0:1, 32:160],
                                                PS[5][0:1, 0:128]),
                        waits=B[5].cons_waits(), inc=1, tag="invnk row ev")
            B[5].add_cons("V", v_cs)
            t_cs2 = mm_group(B[5], [(ONES[0:1, 0:128], SML[0:1, 32:160])],
                             out_ap=PS[5][:, 0:128],
                             extra_waits=[("V", v_cs)], tag="col bcast")
            v_op(lambda v: v.tensor_scalar(ATT[:, :], GT[:, 0:128],
                                           SML[:, 7:8], None, ALU.mult),
                 waits=[("V", v_rs)] + B[5].cons_waits(), tag="sim row")
            v_simc = v_op(lambda v: v.tensor_mul(ATT2[:, :], ATT[:, :],
                                                 PS[5][:, 0:128]),
                          inc=1, tag="sim col")
            B[5].add_cons("V", v_simc)
            v_op(lambda v: v.tensor_add(ATT[:, :], ATT2[:, :], MADD[:, :]),
                 tag="sim mask")
            v_mx = v_op(lambda v: v.reduce_max(SML[:, 0:1], ATT[:, :],
                                               axis=AX.X, negate=True),
                        inc=1, tag="att max")
            s_ex = s_op(lambda s: s.activation(ATT2[:, :], ATT[:, :], AF.Exp,
                                               bias=SML[:, 0:1]),
                        waits=[("V", v_mx)], inc=1, tag="att exp")
            v_op(lambda v: v.reduce_sum(SML[:, 1:2], ATT2[:, :], axis=AX.X),
                 waits=[("S", s_ex)], tag="att sum")
            v_op(lambda v: v.reciprocal(SML[:, 8:9], SML[:, 1:2]),
                 tag="att rec")
            v_att = v_op(lambda v: v.tensor_scalar(ATT[:, :], ATT2[:, :],
                                                   SML[:, 8:9], None,
                                                   ALU.mult),
                         inc=1, tag="att norm")
            t_atp = mm_group(B[4], [(ATT[:, :], IDF[:, :])],
                             out_ap=PS[4][:, 0:128],
                             extra_waits=[("V", v_att)], tag="att tp")
            s_atb = s_op(lambda s: s.copy(ATB[:, :], PS[4][:, 0:128]),
                         waits=B[4].cons_waits(), inc=1, tag="atb")
            B[4].add_cons("S", s_atb)

            tpk = 0
            for dc in range(8):
                w = ([slot_load["B"], ("V", vprfb)] + B[2].prod_waits()
                     if dc == 0 else [])
                for ec in range(8):
                    tpk = t_mm(PS[2][:, dc:dc + 1],
                               SLOTB[:, ec * D + dc * 128:
                                     ec * D + (dc + 1) * 128],
                               PRFB[:, ec:ec + 1], ec == 0, ec == 7,
                               waits=w if ec == 0 else [],
                               inc=1 if (dc == 7 and ec == 7) else 0,
                               tag=f"pk {dc}.{ec}")
            B[2].set_prod(tpk)
            v_pk = v_op(lambda v: v.tensor_copy(SML[:, 16:24], PS[2][:, 0:8]),
                        waits=B[2].cons_waits(), inc=1, tag="pk ev")
            v_pkb = v_op(lambda v: v.tensor_copy(PRFB[:, :], SML[:, 16:24]),
                         inc=1, tag="pk bf")
            B[2].add_cons("V", v_pkb)
            tinit = 0
            for kc in range(4):
                w = ([("V", v_pkb)] + B[2].prod_waits() if kc == 0 else [])
                for dc in range(8):
                    tinit = t_mm(PS[2][:, 8 + kc:9 + kc],
                                 SLOTD[:, dc * 512 + kc * 128:
                                       dc * 512 + (kc + 1) * 128],
                                 PRFB[:, dc:dc + 1], dc == 0, dc == 7,
                                 waits=w if dc == 0 else [],
                                 inc=1 if (kc == 3 and dc == 7) else 0,
                                 tag=f"initf {kc}.{dc}")
            B[2].set_prod(tinit)
            v_initf = v_op(lambda v: v.tensor_copy(SML[:, 24:28],
                                                   PS[2][:, 8:12]),
                           waits=B[2].cons_waits(), inc=1, tag="initf ev")
            B[2].add_cons("V", v_initf)
            exch["v_initf"] = v_initf
            exch["v_pkb"] = v_pkb

            s_oatt = 0
            for j in range(4):
                bank = B[[0, 1, 4, 5][j % 4]]
                mm_group(bank, [(ATB[:, :], VFM[:, j * 512:(j + 1) * 512])],
                         extra_waits=[("S", s_atb), ("S", s_vfm)],
                         tag=f"oatt {j}")
                s_oatt = s_op(lambda s, j=j, bank=bank: s.copy(
                    OATT[:, j * 512:(j + 1) * 512], bank.ap[:, :]),
                    waits=bank.cons_waits(), inc=1, tag=f"oatt ev {j}")
                bank.add_cons("S", s_oatt)
            exch["s_oatt"] = s_oatt
            slot_load["B"] = g_dma(SLOTB[:, :], scr_m2[:, :],
                                   waits=[("T", tpk), g_m_done],
                                   tag="t0 B<-m2")

        if debug:
            y_dma(dbg["xnfm"], XNFM[:, :], waits=[("V", xnfm_ev[NB - 1])],
                  tag="dbg xnfm")
            y_dma(dbg["grd"], GRD[:, :], waits=[("S", s_r)], tag="dbg grd")
            y_dma(dbg["xnt"], XNT[:, :], waits=[("V", xnfm_ev[NB - 1])],
                  tag="dbg xnt")
            y_dma(dbg["gt"], GT[:, :], waits=[("V", vprfb)], tag="dbg gt")
            y_dma(dbg["prf"], PRF[:, :], waits=[("V", vprfb)], tag="dbg prf")
            g_dma(dbg["m1"], scr_m1[:, :], waits=[("G0", p.count["G0"])],
                  tag="dbg m1")
            g_dma(dbg["m2"], scr_m2[:, :], waits=[("G0", p.count["G0"])],
                  tag="dbg m2")

        out_guard = {0: ("Y2", 0), 1: ("Y2", 0)}
        xb_guard = {0: xc_guard[0], 1: xc_guard[1]}
        v_initi = 0
        for t in range(NT):
            qphys = "A" if t % 2 == 0 else "B"
            kphys = "B" if t % 2 == 0 else "A"
            qslot = bigslot[qphys]
            kslot = bigslot[kphys]
            scan_mark = {"f": [0] * 4, "i": [0] * 4}
            if t == 0:
                s_qk = s_qk0
            else:
                _, tq_b = emit_half(t, "Q", qphys)
                s_qk, tk_b = emit_half(t, "K", kphys)
                slot_load[kphys] = g_dma(kslot[:, :], scr_m2[:, :],
                                         waits=[("T", tk_b), g_m_done],
                                         tag=f"t{t} {kphys}<-m2")

            def yw_scan(br, kc):
                bank = B[2 + (kc % 2)]
                pairs = [(SLOTD[:, dc * 512 + kc * 128:
                                dc * 512 + (kc + 1) * 128],
                          KFM[:, dc * TL:(dc + 1) * TL]) for dc in range(8)]
                mm_group(bank, pairs,
                         extra_waits=[("S", s_qk), slot_load["D"]],
                         tag=f"t{t} yW{br}{kc}")
                CUM = CUMF if br == "f" else CUMI
                if t == 0:
                    ia = 0.0
                    wi = []
                else:
                    ia = CUM[:, kc * TL + TL - 1: kc * TL + TL]
                    wi = []
                sv = v_op(lambda v, CUM=CUM, kc=kc, bank=bank, ia=ia:
                          v.tensor_tensor_scan(
                              CUM[:, kc * TL:(kc + 1) * TL], ONES[:, :],
                              bank.ap[:, :], ia, ALU.mult, ALU.add),
                          waits=bank.cons_waits() + wi + [cY],
                          inc=1, tag=f"t{t} scan{br}{kc}")
                bank.add_cons("V", sv)
                scan_mark[br][kc] = sv

            tlg = [0]

            def h_group(br, fc, slot, sl_key):
                bank = B[[0, 1, 5][fc % 3]]
                pairs = [(slot[:, dc * D + fc * 128: dc * D + (fc + 1) * 128],
                          QFM[:, dc * TL:(dc + 1) * TL]) for dc in range(8)]
                mm_group(bank, pairs, extra_waits=[slot_load[sl_key]],
                         tag=f"t{t} h{br}{fc}")
                hh = HCH[:, (fc % 2) * 512:(fc % 2 + 1) * 512]
                ev = s_op(lambda s, hh=hh, bank=bank, fc=fc: s.activation(
                    hh, bank.ap[:, :], AF.Gelu, bias=B1G[:, fc:fc + 1]),
                    waits=bank.cons_waits() + [cY] +
                          ([("T", tlg[0])] if fc >= 2 else []),
                    inc=1, tag=f"t{t} gelu{br}{fc}")
                bank.add_cons("S", ev)
                lgw = [("S", ev)]
                if fc == 0:
                    lgw += B[4].prod_waits() + [d_rw2]
                tlg[0] = t_mm(PS[4][0:8, :], RW2B[:, fc * 8:(fc + 1) * 8],
                              hh, fc == 0, fc == 7, waits=lgw,
                              inc=1 if fc == 7 else 0, tag=f"t{t} lg{br}{fc}")
                if fc == 7:
                    B[4].set_prod(tlg[0])

            def xv_group(br, kc):
                bank = B[kc % 2]
                pairs = [(SLOTC[:, dc * 512 + kc * 128:
                                dc * 512 + (kc + 1) * 128],
                          QFM[:, dc * TL:(dc + 1) * TL]) for dc in range(8)]
                mm_group(bank, pairs, extra_waits=[slot_load["C"]],
                         tag=f"t{t} xV{br}{kc}")
                return bank

            for i in range(8):
                if i < 4:
                    yw_scan("f", i)
                h_group("f", i, qslot, qphys)
            t_hf = p.count["T"]
            vlg_f = v_op(lambda v: v.tensor_scalar(
                LG[0:8, :], PS[4][0:8, :], BRW[0:8, 0:1], None, ALU.add),
                waits=B[4].cons_waits() + [("V", v_brw)], inc=1,
                tag=f"t{t} lgf ev")
            B[4].add_cons("V", vlg_f)
            if t == 0:
                emit_exchange_T()
                for kc in range(4):
                    sv = v_op(lambda v, kc=kc: v.tensor_scalar(
                        CUMF[:, kc * TL:(kc + 1) * TL],
                        CUMF[:, kc * TL:(kc + 1) * TL],
                        SML[:, 24 + kc:25 + kc], None, ALU.add),
                        waits=[("V", exch["v_initf"]),
                               ("V", scan_mark["f"][kc])],
                        inc=1, tag=f"t0 addinitf{kc}")
                    scan_mark["f"][kc] = sv
            slot_load["D"] = g_dma(SLOTD[:, :], scr_m3[:, :],
                                   waits=[("T", p.count["T"]), g_m_done],
                                   tag=f"t{t} D<-m3")
            if t == 0:
                tini = 0
                for kc in range(4):
                    w = ([slot_load["D"], ("V", exch["v_pkb"])] +
                         B[2].prod_waits() if kc == 0 else [])
                    for dc in range(8):
                        tini = t_mm(PS[2][:, 12 + kc:13 + kc],
                                    SLOTD[:, dc * 512 + kc * 128:
                                          dc * 512 + (kc + 1) * 128],
                                    PRFB[:, dc:dc + 1], dc == 0, dc == 7,
                                    waits=w if dc == 0 else [],
                                    inc=1 if (kc == 3 and dc == 7) else 0,
                                    tag=f"initi {kc}.{dc}")
                B[2].set_prod(tini)
                v_initi = v_op(lambda v: v.tensor_copy(
                    SML[:, 28:32], PS[2][:, 12:16]),
                    waits=B[2].cons_waits(), inc=1, tag="initi ev")
                B[2].add_cons("V", v_initi)

            def softmax_T(br, blk, vlg, sxx_prev):
                w = [("V", vlg)] + (B[4].prod_waits() if blk == 0 else
                                    [("S", sxx_prev)])
                ttp = t_mm(PS[4][:, 128 + blk * 8:136 + blk * 8],
                           LG[0:8, blk * 128:(blk + 1) * 128],
                           IDF[0:8, 0:8], True, True,
                           waits=w, inc=1, tag=f"t{t} lgtp{br}{blk}")
                vmx = v_op(lambda v, blk=blk: v.reduce_max(
                    WTM[:, 8 + blk:9 + blk],
                    PS[4][:, 128 + blk * 8:136 + blk * 8],
                    axis=AX.X, negate=True),
                    waits=[("T", ttp)], inc=1, tag=f"t{t} wmax{br}{blk}")
                sxx = s_op(lambda s, blk=blk: s.activation(
                    WTM[:, 0:8], PS[4][:, 128 + blk * 8:136 + blk * 8],
                    AF.Exp, bias=WTM[:, 8 + blk:9 + blk]),
                    waits=[("V", vmx)] +
                          ([("T", tpb[0])] if blk else []),
                    inc=1, tag=f"t{t} wexp{br}{blk}")
                v_op(lambda v: v.reduce_sum(WTM[:, 12:13], WTM[:, 0:8],
                                            axis=AX.X),
                     waits=[("S", sxx)], tag=f"t{t} wsum{br}{blk}")
                v_op(lambda v: v.reciprocal(WTM[:, 13:14], WTM[:, 12:13]),
                     tag=f"t{t} wrec{br}{blk}")
                vw = v_op(lambda v, blk=blk, t=t: v.tensor_scalar(
                    WTM[:, 24 + blk * 8:32 + blk * 8], WTM[:, 0:8],
                    WTM[:, 13:14],
                    RCP[:, t * 4 + blk:t * 4 + blk + 1],
                    ALU.mult, ALU.mult),
                    inc=1, tag=f"t{t} wnorm{br}{blk}")
                return vw, sxx

            def softmax_TP(br, blk, vw):
                w = [("V", vw)] + (B[5].prod_waits() if blk == 0 else [])
                tpb[0] = t_mm(PS[5][0:8, blk * 128:(blk + 1) * 128],
                              WTM[:, 24 + blk * 8:32 + blk * 8],
                              IDF[:, :], True, True,
                              waits=w, inc=1, tag=f"t{t} wtpb{br}{blk}")
                if blk == 3:
                    B[5].set_prod(tpb[0])

            tpb = [0]
            vws = []
            sxx_last = 0
            for blk in range(4):
                vw, sxx_last = softmax_T("f", blk, vlg_f, sxx_last)
                vws.append(vw)
                bank_x = xv_group("f", blk)
                vp1 = v_op(lambda v, bank_x=bank_x, kc=blk:
                           v.tensor_mul(PTMP[:, kc * TL:(kc + 1) * TL],
                                        bank_x.ap[:, :],
                                        CUMF[:, kc * TL:(kc + 1) * TL]),
                           waits=bank_x.cons_waits() +
                                 [("V", scan_mark["f"][blk])],
                           inc=1, tag=f"t{t} prod1f{blk}")
                bank_x.add_cons("V", vp1)
            B[4].add_cons("S", sxx_last)
            for blk in range(4):
                softmax_TP("f", blk, vws[blk])
                yw_scan("i", blk)
            if t == 0:
                for kc in range(4):
                    sv = v_op(lambda v, kc=kc: v.tensor_scalar(
                        CUMI[:, kc * TL:(kc + 1) * TL],
                        CUMI[:, kc * TL:(kc + 1) * TL],
                        SML[:, 28 + kc:29 + kc], None, ALU.add),
                        waits=[("V", v_initi), ("V", scan_mark["i"][kc])],
                        inc=1, tag=f"t0 addiniti{kc}")
                    scan_mark["i"][kc] = sv
            v1f = v_op(lambda v: v.tensor_copy(WFM32[0:8, :],
                                               PS[5][0:8, 0:512]),
                       waits=B[5].cons_waits(), inc=1, tag=f"t{t} wfm32 f")
            v2f = v_op(lambda v: v.tensor_copy(WFMB[0:8, 0:TL],
                                               PS[5][0:8, 0:512]),
                       inc=1, tag=f"t{t} wfmb f")
            B[5].add_cons("V", v2f)
            slot_load[qphys] = g_dma(qslot[:, :], scr_wo[:, :],
                                     waits=[("T", t_hf), d_wo],
                                     tag=f"t{t} {qphys}<-wo")
            slot_load["C"] = g_dma(SLOTC[:, :], scr_m1[:, :],
                                   waits=[("T", p.count["T"]), g_m_done],
                                   tag=f"t{t} C<-m1")

            for kc in range(4):
                bank_w = B[4 if kc % 2 == 0 else 5]
                mm_group(bank_w, [(ESEL[0:8, kc * 128:(kc + 1) * 128],
                                   WFM32[0:8, :])],
                         extra_waits=[("V", v1f), cY],
                         tag=f"t{t} wexpf{kc}")
                vp2 = v_op(lambda v, kc=kc, bank_w=bank_w:
                           v.tensor_mul(PRODF[:, kc * TL:(kc + 1) * TL],
                                        PTMP[:, kc * TL:(kc + 1) * TL],
                                        bank_w.ap[:, :]),
                           waits=bank_w.cons_waits(), inc=1,
                           tag=f"t{t} prod2f{kc}")
                bank_w.add_cons("V", vp2)
            for i in range(8):
                h_group("i", i, kslot, kphys)
            t_hi = p.count["T"]
            vlg_i = v_op(lambda v: v.tensor_scalar(
                LG[0:8, :], PS[4][0:8, :], BRW[0:8, 0:1], None, ALU.add),
                waits=B[4].cons_waits() + [("V", v_brw)], inc=1,
                tag=f"t{t} lgi ev")
            B[4].add_cons("V", vlg_i)
            if t < NT - 1:
                slot_load[kphys] = g_dma(kslot[:, :], scr_rw1[:, :],
                                         waits=[("T", t_hi), d_rw1dump],
                                         tag=f"t{t} {kphys}<-rw1'")

            tpb = [0]
            vws = []
            sxx_last = 0
            for blk in range(4):
                vw, sxx_last = softmax_T("i", blk, vlg_i, sxx_last)
                vws.append(vw)
                bank_x = xv_group("i", blk)
                vp1 = v_op(lambda v, bank_x=bank_x, kc=blk:
                           v.tensor_mul(PTMP[:, kc * TL:(kc + 1) * TL],
                                        bank_x.ap[:, :],
                                        CUMI[:, kc * TL:(kc + 1) * TL]),
                           waits=bank_x.cons_waits() +
                                 [("V", scan_mark["i"][blk])],
                           inc=1, tag=f"t{t} prod1i{blk}")
                bank_x.add_cons("V", vp1)
            B[4].add_cons("S", sxx_last)
            for blk in range(4):
                softmax_TP("i", blk, vws[blk])
            v1i = v_op(lambda v: v.tensor_copy(WFM32[0:8, :],
                                               PS[5][0:8, 0:512]),
                       waits=B[5].cons_waits(), inc=1, tag=f"t{t} wfm32 i")
            v2i = v_op(lambda v: v.tensor_copy(WFMB[0:8, TL:2 * TL],
                                               PS[5][0:8, 0:512]),
                       inc=1, tag=f"t{t} wfmb i")
            B[5].add_cons("V", v2i)
            for kc in range(4):
                bank_w = B[4 if kc % 2 == 0 else 5]
                mm_group(bank_w, [(ESEL[0:8, kc * 128:(kc + 1) * 128],
                                   WFM32[0:8, :])],
                         extra_waits=[("V", v1i), cY],
                         tag=f"t{t} wexpi{kc}")
                vp2 = v_op(lambda v, kc=kc, bank_w=bank_w:
                           v.scalar_tensor_tensor(
                               PRODI[:, kc * TL:(kc + 1) * TL],
                               bank_w.ap[:, :], ALPH[:, 0:1],
                               PTMP[:, kc * TL:(kc + 1) * TL],
                               ALU.mult, ALU.mult),
                           waits=bank_w.cons_waits() + [cY], inc=1,
                           tag=f"t{t} prod2i{kc}")
                bank_w.add_cons("V", vp2)
            w_done_i = p.count["V"]
            slot_load["C"] = g_dma(SLOTC[:, :], scr_ui[:, :],
                                   waits=[("T", p.count["T"]), d_ui],
                                   tag=f"t{t} C<-ui")
            slot_load["D"] = g_dma(SLOTD[:, :], scr_uf[:, :],
                                   waits=[("T", p.count["T"]), d_uf],
                                   tag=f"t{t} D<-uf")

            s_pre = 0
            for fc in range(8):
                bank = B[4 if fc % 2 == 0 else 5]
                pairs = []
                for dc in range(4):
                    pairs.append((SLOTD[:, dc * D + fc * 128:
                                        dc * D + (fc + 1) * 128],
                                  PRODF[:, dc * TL:(dc + 1) * TL]))
                pairs.append((BFT[0:8, fc * 128:(fc + 1) * 128],
                              WFMB[0:8, 0:TL]))
                for dc in range(4):
                    pairs.append((SLOTC[:, dc * D + fc * 128:
                                        dc * D + (fc + 1) * 128],
                                  PRODI[:, dc * TL:(dc + 1) * TL]))
                pairs.append((BIT[0:8, fc * 128:(fc + 1) * 128],
                              WFMB[0:8, TL:2 * TL]))
                xw = [slot_load["D"], slot_load["C"],
                      ("V", w_done_i), d_bf, ("V", v_bi)]
                mm_group(bank, pairs, extra_waits=xw, tag=f"t{t} pre{fc}")
                s_pre = s_op(lambda s, fc=fc, bank=bank: s.copy(
                    PREB[:, fc * TL:(fc + 1) * TL], bank.ap[:, :]),
                    waits=bank.cons_waits(), inc=1, tag=f"t{t} pre{fc} ev")
                bank.add_cons("S", s_pre)

            xbl = [XBLK0, XBLK1]
            for blk in range(4):
                dxb = y_dma(xbl[blk % 2][:, :],
                            x_in[t * TL + blk * 128:
                                 t * TL + (blk + 1) * 128, :],
                            waits=[("V", xb_guard[blk % 2])],
                            tag=f"t{t} xres{blk}",
                            ring="Y1" if blk % 2 == 0 else "Y3")
                evv2 = 0
                dovs = []
                for half in range(2):
                    bank = B[[4, 5, 0, 1][(blk * 2 + half) % 4]]
                    pairs = []
                    for fc in range(8):
                        pairs.append((PREB[:, fc * TL + blk * 128:
                                           fc * TL + (blk + 1) * 128],
                                      qslot[:, fc * D + half * 512:
                                            fc * D + (half + 1) * 512]))
                    pairs.append((OATT[:, t * TL + blk * 128:
                                       t * TL + (blk + 1) * 128],
                                  WOUT[:, half * 512:(half + 1) * 512]))
                    xw = [("S", s_pre), slot_load[qphys],
                          d_wout, ("S", exch["s_oatt"])]
                    mm_group(bank, pairs, extra_waits=xw,
                             tag=f"t{t} out{blk}.{half}")
                    ob = OUTB0 if blk % 2 == 0 else OUTB1
                    evv2 = v_op(lambda v, ob=ob, half=half, bank=bank,
                                xb=xbl[blk % 2]:
                                v.tensor_add(
                                    ob[:, half * 512:(half + 1) * 512],
                                    bank.ap[:, :],
                                    xb[:, half * 512:(half + 1) * 512]),
                                waits=bank.cons_waits() + [dxb] +
                                      [out_guard[blk % 2]],
                                inc=1, tag=f"t{t} outev{blk}.{half}")
                    bank.add_cons("V", evv2)
                    dovs.append(y_dma(
                        out_d[t * TL + blk * 128:t * TL + (blk + 1) * 128,
                              half * 512:(half + 1) * 512],
                        (OUTB0 if blk % 2 == 0 else OUTB1)[
                            :, half * 512:(half + 1) * 512],
                        waits=[("V", evv2)], tag=f"t{t} outdma{blk}.{half}",
                        ring="Y2" if half == 0 else "Y0"))
                xb_guard[blk % 2] = evv2
                out_guard[blk % 2] = dovs
            if t < NT - 1:
                slot_load[qphys] = g_dma(qslot[:, :], scr_wk[:, :],
                                         waits=[("T", p.count["T"]), d_wk],
                                         tag=f"t{t} {qphys}<-wk'")
                slot_load["C"] = g_dma(SLOTC[:, :], scr_vf[:, :],
                                       waits=[("T", p.count["T"]), d_vf],
                                       tag=f"t{t} C<-vf'")
                slot_load["D"] = g_dma(SLOTD[:, :], scr_wf[:, :],
                                       waits=[("T", p.count["T"]), d_wf],
                                       tag=f"t{t} D<-wf'")

            if debug and t == 0:
                y_dma(dbg["qfm"], QFM[:, :], waits=[("S", s_qk)], tag="dbg q")
                y_dma(dbg["kfm"], KFM[:, :], waits=[("S", s_qk)], tag="dbg k")
                y_dma(dbg["cumf"], CUMF[:, :],
                      waits=[("V", scan_mark["f"][3])], tag="dbg cumf")
                y_dma(dbg["cumi"], CUMI[:, :],
                      waits=[("V", scan_mark["i"][3])], tag="dbg cumi")
                y_dma(dbg["wfm"], WFMB[0:8, :], waits=[("V", v2i)],
                      tag="dbg wfm")
                y_dma(dbg["preb"], PREB[:, :], waits=[("S", s_pre)],
                      tag="dbg preb")
                y_dma(dbg["initf"], SML[:, 16:32], waits=[("V", v_initi)],
                      tag="dbg initf")

        p.emit("G", lambda g: g.nop(),
               waits=[(s, p.count[s]) for s in
                      ("Y0", "Y1", "Y2", "Y3", "G0", "G1", "G2", "G3")],
               tag="final wait")

        block = st.enter_context(nc.Block())

        def make_runner(eng_key):
            def run(eng):
                for idx, (waits, fn, inc, sem, tag) in enumerate(p.ops[eng_key]):
                    for sk, val in waits:
                        eng.wait_ge(sems[sk], val)
                    inst = fn(eng)
                    drained = (eng_key == "V" or
                               (eng_key == "S" and
                                (idx < S_DRAIN_LIMIT if S_DRAIN_SET is None
                                 else idx in S_DRAIN_SET)))
                    if drained:
                        d = eng.drain()
                        if inc:
                            d.then_inc(sems[sem], inc)
                    elif inc:
                        inst.then_inc(sems[sem], inc)
            return run

        block.tensor(make_runner("T"))
        block.vector(make_runner("V"))
        block.scalar(make_runner("S"))
        block.gpsimd(make_runner("G"))
        block.sync(make_runner("Y"))

    return nc


def _img8(w):
    c = w.shape[0] // 128
    return np.ascontiguousarray(
        w.reshape(c, 128, -1).transpose(1, 0, 2).reshape(128, -1))


def _madd():
    m = np.full((128, 128), -1e30, np.float32)
    for h in range(8):
        m[h * 16:(h + 1) * 16, h * 16:(h + 1) * 16] = 0.0
    return m


def _esel():
    e = np.zeros((8, 512), np.float32)
    for kc in range(4):
        for pp in range(128):
            e[2 * kc + pp // 64, kc * 128 + pp] = 1.0
    return e


_cache = {}


def _get_nc(debug=False):
    key = ("nc", debug, S_DRAIN_LIMIT, tuple(sorted(S_DRAIN_SET)) if S_DRAIN_SET else None)
    if key not in _cache:
        _cache[key] = build_nc(debug)
    return _cache[key]


def make_in_maps(inputs):
    f32 = np.float32
    x = np.asarray(inputs["L0_attn_x"], f32).reshape(T_FULL, D)
    gamma = np.asarray(inputs["L0_attn_norm_weight"], f32)
    w_qkv = np.asarray(inputs["L0_attn_w_qkv"], f32)
    temp = np.asarray(inputs["L0_attn_temperature"], f32).reshape(8)
    w_out = np.asarray(inputs["L0_attn_w_out"], f32)
    W_Q = np.asarray(inputs["L0_ffn_W_Q"], f32)
    W_K = np.asarray(inputs["L0_ffn_W_K"], f32)
    W_O = np.asarray(inputs["L0_ffn_W_O"], f32)
    W_inv = np.asarray(inputs["L0_ffn_W_inv"], f32)
    V_f = np.asarray(inputs["L0_ffn_V_fwd"], f32)
    W_f = np.asarray(inputs["L0_ffn_W_fwd"], f32)
    U_f = np.asarray(inputs["L0_ffn_U_fwd"], f32)
    b_f = np.asarray(inputs["L0_ffn_b_fwd"], f32)
    V_i = np.asarray(inputs["L0_ffn_V_inv"], f32)
    WIE = np.asarray(inputs["L0_ffn_W_inv_exp"], f32)
    U_i = np.asarray(inputs["L0_ffn_U_inv"], f32)
    b_i = np.asarray(inputs["L0_ffn_b_inv"], f32)
    rw1 = np.asarray(inputs["L0_ffn_router_w1"], f32)
    rb1 = np.asarray(inputs["L0_ffn_router_b1"], f32)
    rw2 = np.asarray(inputs["L0_ffn_router_w2"], f32)
    rb2 = np.asarray(inputs["L0_ffn_router_b2"], f32)
    alpha = np.asarray(inputs["L0_ffn_alpha_bi"], f32).reshape(())
    ebias = np.asarray(inputs["L0_ffn_expert_bias"], f32)

    shared = {
        "wq_im": _img8(np.ascontiguousarray(W_Q.T)),
        "wk_im": _img8(np.ascontiguousarray(W_K.T)),
        "wo_im": _img8(np.ascontiguousarray(W_O.T)),
        "rw1_im": _img8(np.ascontiguousarray(rw1.T)),
        "winv_im": _img8(W_inv),
        "vf_im": _img8(V_f.transpose(1, 0, 2).reshape(D, KR)),
        "wf_im": _img8(W_f.transpose(1, 0, 2).reshape(D, KR)),
        "uf_im": _img8(U_f.transpose(0, 2, 1).reshape(KR, D)),
        "wie_im": _img8(WIE.transpose(1, 0, 2).reshape(D, KR)),
        "vi_im": _img8(V_i.transpose(1, 0, 2).reshape(D, KR)),
        "ui_im": _img8(U_i.transpose(0, 2, 1).reshape(KR, D)),
        "wqkv_im": _img8(np.ascontiguousarray(w_qkv.T)),
        "wout_im": np.ascontiguousarray(w_out.T),
        "bf_im": np.ascontiguousarray(b_f),
        "bi_im": np.ascontiguousarray(b_i),
        "rw2_im": _img8(np.ascontiguousarray(rw2.T)),
        "gam": np.ascontiguousarray(gamma.reshape(8, 128).T),
        "b1g": np.ascontiguousarray(rb1.reshape(8, 128).T),
        "tmpr": np.ascontiguousarray(np.repeat(temp, 16).reshape(128, 1)),
        "alph": np.ascontiguousarray(
            np.broadcast_to(alpha, (128, 1)).astype(f32)),
        "b2c": np.ascontiguousarray(rb2.reshape(8, 1)),
        "ebc": np.ascontiguousarray(ebias.reshape(8, 1)),
        "madd": _madd(),
        "idf": np.eye(128, dtype=f32),
        "idb": np.eye(128, dtype=BF),
        "onef": np.ones((128, 512), f32),
        "onesb": np.ones((128, 512), BF),
        "esel": _esel(),
    }
    in_maps = []
    for c in range(N_CORES):
        m = dict(shared)
        m["x"] = np.ascontiguousarray(x[c * L:(c + 1) * L])
        pm = np.zeros((128, 8), f32)
        pm[:, :c] = 1.0
        m["pmsk"] = pm
        gidx = c * L + np.arange(L).reshape(NB, 128)
        m["rcp"] = np.ascontiguousarray((1.0 / (gidx + 1)).astype(f32).T)
        in_maps.append(m)
    return in_maps


def kernel(**inputs):
    nc = _get_nc(DEBUG)
    in_maps = make_in_maps(inputs)
    res = run_bass_kernel_spmd(nc, in_maps, core_ids=list(range(N_CORES)))
    out = np.concatenate([res.results[c]["out"] for c in range(N_CORES)],
                         axis=0)
    if DEBUG:
        kernel.last_results = res
    return out.reshape(1, T_FULL, D).astype(np.float32)
